# version 14
# speedup vs baseline: 1.1277x; 1.0335x over previous
import math
from contextlib import ExitStack

import numpy as np
import ml_dtypes
import concourse.bass as bass
import concourse.mybir as mybir
from concourse.bass_utils import run_bass_kernel_spmd

F32 = mybir.dt.float32
BF16 = mybir.dt.bfloat16
I32 = mybir.dt.int32
AF = mybir.ActivationFunctionType
ALU = mybir.AluOpType
AX = mybir.AxisListType
ET = mybir.EngineType

BF = ml_dtypes.bfloat16

B, S, D = 2, 8192, 1024
NCORE = 8
GSZ = 4
TPC = 2048
NP = 16
DC = 8
GROUPS = [[0, 1, 2, 3], [4, 5, 6, 7]]
NEG = -1e30

ENGS = ("sync", "tensor", "vector", "scalar", "gpsimd")


class P:

    def __init__(self, nc):
        self.nc = nc
        self.ops = {e: [] for e in ENGS}
        self.cnt = {}
        self.regions = {}
        self.wm = {e: {} for e in ENGS}
        self.sbuf_bytes = 0
        self._sb = {}
        self.prefix = ""
        self._stage_guards = []

    def sbuf(self, name, shape, dtype, persistent=False):
        name = self.prefix + name
        g = self.nc.sbuf_tensor(name, shape, dtype)
        t = g.__enter__()
        if not persistent:
            self._stage_guards.append(g)
        per_part = int(np.prod(shape[1:])) * mybir.dt.size(dtype)
        self.sbuf_bytes += per_part
        self._sb[name] = t
        return t

    def psum(self, name, shape, dtype=F32, persistent=False):
        name = self.prefix + name
        g = self.nc.psum_tensor(name, shape, dtype)
        t = g.__enter__()
        if not persistent:
            self._stage_guards.append(g)
        return t

    def dram(self, name, shape, dtype, kind="Internal"):
        return self.nc.dram_tensor(self.prefix + name, shape, dtype, kind=kind)

    def stage_end(self):
        snap = dict(self.cnt)
        for eng in ENGS:
            waits = []
            for s, v in snap.items():
                if self.wm[eng].get(s, 0) < v:
                    waits.append((s, v))
                    self.wm[eng][s] = v
            if waits:
                self.ops[eng].append((None, waits, None, 0))
        self.regions = {}
        self.sbuf_bytes = 0
        for g in reversed(self._stage_guards):
            g.__exit__(None, None, None)
        self._stage_guards = []

    def batch(self, eng):
        p = self

        class _B:
            def __enter__(s):
                assert getattr(p, "_batch", None) is None
                p._batch = (eng, [])
                return s

            def __exit__(s, *a):
                beng, items = p._batch
                p._batch = None
                if not items:
                    return False
                rd, wr, seen_r, seen_w = [], [], set(), set()
                for f, r, w in items:
                    for x in r:
                        if x not in seen_r:
                            seen_r.add(x)
                            rd.append(x)
                    for x in w:
                        if x not in seen_w:
                            seen_w.add(x)
                            wr.append(x)
                fns = [f for f, _, _ in items]

                def merged(t, fns=fns):
                    last = None
                    for f in fns:
                        last = f(t)
                    return last
                p.emit(beng, merged, reads=rd, writes=wr)
                return False
        return _B()

    def emit(self, eng, fn, reads=(), writes=(), dma=False, coll=False):
        if getattr(self, "_batch", None) is not None and not dma and not coll:
            beng, items = self._batch
            assert eng == beng, f"emit on {eng} inside batch({beng})"
            items.append((fn, list(reads), list(writes)))
            return None
        if coll:
            self._collseq = getattr(self, "_collseq", 0) + 1
            sem = f"coll{self._collseq}"
        else:
            sem = eng + ("D" if dma else "C")
        need = {}

        def add(dep):
            s, v = dep
            if s.endswith("D"):
                v = self.cnt.get(s, v)
            if need.get(s, 0) < v:
                need[s] = v

        reads = [self.prefix + r for r in reads]
        writes = [self.prefix + w for w in writes]
        for r in reads:
            st = self.regions.get(r)
            if st is not None and st["w"] is not None:
                add(st["w"])
        for w in writes:
            st = self.regions.get(w)
            if st is not None:
                if st["w"] is not None:
                    add(st["w"])
                for rd in st["r"]:
                    add(rd)
        if dma and not coll:
            prev = self.cnt.get(sem, 0)
            if prev and need.get(sem, 0) < prev:
                need[sem] = prev
        waits = []
        for s, v in need.items():
            if eng == "tensor" and s == "tensorC":
                continue
            if self.wm[eng].get(s, 0) < v:
                waits.append((s, v))
                self.wm[eng][s] = v
        inc = 16 if dma else 1
        self.cnt[sem] = self.cnt.get(sem, 0) + inc
        val = self.cnt[sem]
        self.ops[eng].append((fn, waits, sem, inc))
        for w in writes:
            self.regions[w] = {"w": (sem, val), "r": []}
        for r in reads:
            st = self.regions.setdefault(r, {"w": None, "r": []})
            st["r"].append((sem, val))
        return (sem, val)

    def barrier_from(self, eng, deps):
        waits = []
        for s, v in deps:
            if self.wm[eng].get(s, 0) < v:
                waits.append((s, v))
                self.wm[eng][s] = v
        if waits:
            self.ops[eng].append((None, waits, None, 0))

    def build(self):
        nc = self.nc
        es = ExitStack()
        sems = {}
        for s in sorted(self.cnt):
            sems[s] = es.enter_context(nc.semaphore(s))
        blk = es.enter_context(nc.Block())
        for eng in ENGS:
            if not self.ops[eng]:
                continue
            section = getattr(blk, eng)
            ops = self.ops[eng]

            def body(engobj, ops=ops):
                for fn, waits, sem, inc in ops:
                    for s, v in waits:
                        engobj.wait_ge(sems[s], v)
                    if fn is not None:
                        ins = fn(engobj)
                        if inc:
                            ins.then_inc(sems[sem], inc)
            section(body)
        es.close()



def core_rows(c):
    m = c % 4
    idx = []
    for p in range(NP):
        s0 = 512 * p + 128 * m
        idx.extend(range(s0, s0 + 128))
    return np.array(idx)


def split_x(x):
    out = []
    for c in range(NCORE):
        b = c // 4
        rows = core_rows(c)
        out.append(np.ascontiguousarray(x[b][rows].T))
    return out


def merge_x(slices):
    x = np.zeros((B, S, D), np.float32)
    for c in range(NCORE):
        b = c // 4
        rows = core_rows(c)
        x[b][rows] = slices[c].T
    return x



def _rearr_fm(dram_ap):
    T = dram_ap.shape[1]
    return bass.AP(dram_ap.tensor, 0, [[T, 128], [128 * T, DC], [1, T]])


class MK:

    def __init__(self, nc):
        self.nc = nc
        self.p = P(nc)
        self.uid = 0

    def name(self, base):
        self.uid += 1
        return f"{base}_{self.uid}"

    def constcol(self, value):
        key = f"cc_{value}"
        if not hasattr(self, "_cc"):
            self._cc = {}
        if key not in self._cc:
            t = self.p.sbuf(self.name("cc"), [128, 1], F32, persistent=True)
            self.p.emit("vector", lambda v, t=t, value=value: v.memset(t[:], value),
                        writes=[key])
            self._cc[key] = t
        return self._cc[key]

    def setup_consts(self):
        if hasattr(self, "ones_f"):
            return
        p = self.p
        self.ones_f = p.sbuf("ones_f", [128, 128], F32, persistent=True)
        self.ones_b = p.sbuf("ones_b", [128, 128], BF16, persistent=True)
        self.id_f = p.sbuf("id_f", [128, 128], F32, persistent=True)
        self.id_b = p.sbuf("id_b", [128, 128], BF16, persistent=True)
        self.ones_r = p.sbuf("ones_r", [1, 512], F32, persistent=True)
        p.emit("gpsimd", lambda g: g.memset(self.ones_f[:], 1.0), writes=["ones_f"])
        p.emit("gpsimd", lambda g: g.memset(self.ones_b[:], 1.0), writes=["ones_b"])
        p.emit("gpsimd", lambda g: g.memset(self.ones_r[:], 1.0), writes=["ones_r"])
        p.emit("gpsimd", lambda g: g.memset(self.id_f[:], 0.0), writes=["id_f"])
        p.emit("gpsimd", lambda g: g.affine_select(
            out=self.id_f[:], in_=self.id_f[:], compare_op=ALU.not_equal,
            fill=1.0, base=0, pattern=[[-1, 128]], channel_multiplier=1),
            reads=["id_f"], writes=["id_f"])
        p.emit("gpsimd", lambda g: g.memset(self.id_b[:], 0.0), writes=["id_b"])
        p.emit("gpsimd", lambda g: g.affine_select(
            out=self.id_b[:], in_=self.id_b[:], compare_op=ALU.not_equal,
            fill=1.0, base=0, pattern=[[-1, 128]], channel_multiplier=1),
            reads=["id_b"], writes=["id_b"])

    def layernorm(self, x_sb, xname, xln, lnname, T, ps_sum, ps_sq, psn_sum, psn_sq):
        p = self.p
        scr = getattr(self, "ln_scr", None)
        if scr is None:
            self.ln_scr = p.sbuf("ln_scr", [128, 1024], F32, persistent=True)
            self.ln_A = p.sbuf("ln_A", [128, 2048], F32, persistent=True)
            self.ln_M = p.sbuf("ln_M", [128, 2048], F32, persistent=True)
            self.ln_t = p.sbuf("ln_t", [128, 1024], F32, persistent=True)
        scr, At, Mt, tmp = self.ln_scr, self.ln_A, self.ln_M, self.ln_t
        ntch = T // 512
        assert ntch <= 4
        for tch in range(ntch):
            for dc in range(DC):
                xs = x_sb[:, dc * T + tch * 512: dc * T + tch * 512 + 512]
                p.emit("tensor", lambda t, xs=xs, dc=dc: t.matmul(
                    ps_sum[:, 0:512], self.ones_f[:], xs, start=(dc == 0), stop=(dc == DC - 1)),
                    reads=[xname, "ones_f"], writes=[psn_sum])
                sl = dc % 2
                sq = scr[:, sl * 512:sl * 512 + 512]
                sqn = f"ln_scr{sl}"
                p.emit("scalar", lambda s, xs=xs, sq=sq: s.activation(sq, xs, AF.Square),
                       reads=[xname], writes=[sqn])
                p.emit("tensor", lambda t, sq=sq, dc=dc: t.matmul(
                    ps_sq[:, 0:512], self.ones_f[:], sq, start=(dc == 0), stop=(dc == DC - 1)),
                    reads=[sqn, "ones_f"], writes=[psn_sq])
            Ms = Mt[:, tch * 512: tch * 512 + 512]
            As = At[:, tch * 512: tch * 512 + 512]
            t0 = tmp[:, 0:512]
            t2 = tmp[:, 512:1024]
            p.emit("vector", lambda v, Ms=Ms: v.tensor_scalar(
                Ms, ps_sum[:, 0:512], 1.0 / D, None, op0=ALU.mult),
                reads=[psn_sum], writes=["ln_M"])
            p.emit("vector", lambda v, t0=t0: v.tensor_scalar(
                t0, ps_sq[:, 0:512], 1.0 / D, None, op0=ALU.mult),
                reads=[psn_sq], writes=["ln_ta"])
            p.emit("vector", lambda v, Ms=Ms, t2=t2: v.tensor_tensor(
                t2, Ms, Ms, op=ALU.mult),
                reads=["ln_M"], writes=["ln_tb"])
            p.emit("vector", lambda v, t0=t0, t2=t2: v.tensor_sub(t0, t0, t2),
                   reads=["ln_ta", "ln_tb"], writes=["ln_ta"])
            eps = self.constcol(1e-5)
            p.emit("scalar", lambda s, t0=t0, eps=eps: s.activation(
                t0, t0, AF.Sqrt, bias=eps[:, 0:1]),
                reads=["ln_ta", "cc_1e-05"], writes=["ln_ta"])
            p.emit("vector", lambda v, As=As, t0=t0: v.reciprocal(As, t0),
                   reads=["ln_ta"], writes=["ln_A"])
        for tch in range(ntch):
            Ms = Mt[:, tch * 512: tch * 512 + 512]
            As = At[:, tch * 512: tch * 512 + 512]
            for dc in range(DC):
                xs = x_sb[:, dc * T + tch * 512: dc * T + tch * 512 + 512]
                ys = xln[:, dc * T + tch * 512: dc * T + tch * 512 + 512]
                sl = dc % 2
                t1 = scr[:, sl * 512:sl * 512 + 512]
                t1n = f"ln_scr{sl}"
                p.emit("vector", lambda v, xs=xs, Ms=Ms, t1=t1: v.tensor_sub(t1, xs, Ms),
                       reads=[xname, "ln_M"], writes=[t1n])
                p.emit("vector", lambda v, ys=ys, t1=t1, As=As: v.tensor_tensor(
                    ys, t1, As, op=ALU.mult),
                    reads=[t1n, "ln_A"], writes=[lnname])

    def proj_fm(self, xln, lnname, wsb, wname, T, OUT, psbanks, psnames, evac_cb):
        p = self.p
        nb = len(psbanks)
        i = 0
        for tch in range(T // 512):
            for oc in range(OUT // 128):
                ps, psn = psbanks[i % nb], psnames[i % nb]
                i += 1
                with p.batch("tensor"):
                    for dc in range(DC):
                        lhs = wsb[:, dc * OUT + oc * 128: dc * OUT + oc * 128 + 128]
                        rhs = xln[:, dc * T + tch * 512: dc * T + tch * 512 + 512]
                        p.emit("tensor", lambda t, lhs=lhs, rhs=rhs, dc=dc, ps=ps: t.matmul(
                            ps[:, 0:512], lhs, rhs, start=(dc == 0), stop=(dc == DC - 1)),
                            reads=[wname, lnname], writes=[psn])
                evac_cb(oc, tch, ps, psn)

    def proj_tm(self, xln, lnname, wsb, wname, T, OUT, psbanks, psnames, evac_cb):
        p = self.p
        nb = len(psbanks)
        i = 0
        for tt in range(T // 128):
            for oc in range(OUT // 512):
                ps, psn = psbanks[i % nb], psnames[i % nb]
                i += 1
                with p.batch("tensor"):
                    for dc in range(DC):
                        lhs = xln[:, dc * T + tt * 128: dc * T + tt * 128 + 128]
                        rhs = wsb[:, dc * OUT + oc * 512: dc * OUT + oc * 512 + 512]
                        p.emit("tensor", lambda t, lhs=lhs, rhs=rhs, dc=dc, ps=ps: t.matmul(
                            ps[:, 0:512], lhs, rhs, start=(dc == 0), stop=(dc == DC - 1)),
                            reads=[wname, lnname], writes=[psn])
                evac_cb(tt, oc, ps, psn)


ATT_GROUPS = [(2 * i, 2 * i + 1) for i in range(8)]


def build_attn_a(mk, x_from_dram=True, xin=None, xout=None, wsh=None, wrec=None):
    nc, p = mk.nc, mk.p
    def win(name, shape, dtype=F32):
        if wsh is not None and name in wsh:
            return wsh[name]
        apx = p.dram(name, shape, dtype, kind="ExternalInput").ap()
        if wrec is not None:
            wrec[name] = apx
        return apx

    T = TPC
    x_in = xin if xin is not None else p.dram("x_in", [D, T], F32, kind="ExternalInput").ap()
    wq_d = win("wq", [D, D], F32)
    wk_d = win("wk", [D, D], F32)
    wv_d = win("wv", [D, D], F32)
    mtab_d = win("mtab", [128, 640], F32)
    x_out = xout if xout is not None else p.dram("x_out", [D, T], F32, kind="ExternalOutput").ap()
    kcc = p.dram("kcc", [D, T], BF16).ap()
    vcc = p.dram("vcc", [T, D], BF16).ap()
    kag = p.dram("kag", [4 * D, T], BF16).ap()
    vag = p.dram("vag", [4 * T, D], BF16).ap()
    nkl_d = p.dram("nkl", [1, 1], F32).ap()
    nkr_d = p.dram("nkr", [1, 1], F32).ap()

    x_sb = p.sbuf("x_sb", [128, DC * T], F32)
    xln = p.sbuf("xln", [128, DC * T], BF16)
    q_fm = p.sbuf("q_fm", [128, DC * T], BF16)
    w_sb = p.sbuf("w_sb", [128, DC * D], BF16)
    kv_st = p.sbuf("kv_st", [128, 3 * 512], BF16)
    nq2 = p.sbuf("nq2", [128, 1], F32)
    nk2 = p.sbuf("nk2", [128, 1], F32)
    nred = p.sbuf("nred", [128, 4], F32)
    Bcol = p.sbuf("Bcol", [128, 1], F32)
    ktile = p.sbuf("ktile", [128, 3 * DC * 128], BF16)
    vtile = p.sbuf("vtile", [128, 3 * D], BF16)
    ptile = p.sbuf("ptile", [128, 3 * 256], BF16)
    stf = p.sbuf("stf", [128, 3 * 256], F32)
    tsb = p.sbuf("tsb", [128, 640], F32)
    outn = p.sbuf("outn", [128, 1024], F32)
    rcol = p.sbuf("rcol", [128, 4], F32)

    ps = [p.psum(f"ps{i}", [128, 512]) for i in range(8)]
    psn = [f"ps{i}" for i in range(8)]

    mk.setup_consts()

    p.emit("sync", lambda s: s.dma_start(out=x_sb[:], in_=_rearr_fm(x_in)),
           writes=["x"], dma=True)
    p.emit("sync", lambda s: s.dma_start(out=tsb[:], in_=mtab_d[:]),
           writes=["tsb"], dma=True)

    mk.layernorm(x_sb, "x", xln, "xln", T, ps[0], ps[1], psn[0], psn[1])

    def load_w(dram_ap):
        p.emit("gpsimd", lambda g, d=dram_ap: g.dma_start(
            out=w_sb[:], in_=bass.AP(d.tensor, 0, [[D, 128], [128 * D, DC], [1, D]])),
            writes=["w_sb"], dma=True)

    load_w(wq_d)

    def evac_q(oc, tch, psb, psname):
        dst = q_fm[:, oc * T + tch * 512: oc * T + tch * 512 + 512]
        p.emit("vector", lambda v, dst=dst, psb=psb: v.tensor_copy(dst, psb[:, 0:512]),
               reads=[psname], writes=["q_fm"])
    mk.proj_fm(xln, "xln", w_sb, "w_sb", T, D, ps[0:4], psn[0:4], evac_q)

    for tch in range(4):
        for dc in range(DC):
            sl = dc % 2
            sq = mk.ln_scr[:, sl * 512: sl * 512 + 512]
            src = q_fm[:, dc * T + tch * 512: dc * T + tch * 512 + 512]
            p.emit("scalar", lambda s, sq=sq, src=src: s.activation(sq, src, AF.Square),
                   reads=["q_fm"], writes=[f"ln_scr{sl}"])
            p.emit("tensor", lambda t, sq=sq, dc=dc: t.matmul(
                ps[4][:, 0:512], mk.ones_f[:], sq, start=(dc == 0), stop=(dc == DC - 1)),
                reads=[f"ln_scr{sl}", "ones_f"], writes=[psn[4]])
        p.emit("vector", lambda v, tch=tch: v.reduce_max(
            nred[:, tch:tch + 1], ps[4][:, 0:512], axis=AX.X),
            reads=[psn[4]], writes=["nred"])
    p.emit("vector", lambda v: v.reduce_max(nq2[:, 0:1], nred[:, 0:4], axis=AX.X),
           reads=["nred"], writes=["nq2"])

    load_w(wk_d)

    def evac_k(oc, tch, psb, psname):
        sl = (oc + tch) % 3
        st = kv_st[:, sl * 512: sl * 512 + 512]
        stn = f"kv{sl}"
        p.emit("scalar", lambda s, st=st, psb=psb: s.mul(st, psb[:, 0:512], 1.0 / 32.0),
               reads=[psname], writes=[stn])
        p.emit("sync", lambda s, st=st, oc=oc, tch=tch: s.dma_start(
            out=bass.AP(kcc.tensor, (oc * 128) * T + tch * 512, [[T, 128], [1, 512]]),
            in_=st),
            reads=[stn], writes=[f"kcc{oc}_{tch}"], dma=True)
        ssl = oc % 2
        sq = mk.ln_scr[:, ssl * 512: ssl * 512 + 512]
        p.emit("scalar", lambda s, sq=sq, psb=psb: s.activation(
            sq, psb[:, 0:512], AF.Square, scale=1.0 / 32.0),
            reads=[psname], writes=[f"ln_scr{ssl}"])
        p.emit("tensor", lambda t, sq=sq, oc=oc: t.matmul(
            ps[5][:, 0:512], mk.ones_f[:], sq, start=(oc == 0), stop=(oc == DC - 1)),
            reads=[f"ln_scr{ssl}", "ones_f"], writes=[psn[5]])
        if oc == DC - 1:
            p.emit("vector", lambda v, tch=tch: v.reduce_max(
                nred[:, tch:tch + 1], ps[5][:, 0:512], axis=AX.X),
                reads=[psn[5]], writes=["nred"])
    mk.proj_fm(xln, "xln", w_sb, "w_sb", T, D, ps[0:4], psn[0:4], evac_k)
    p.emit("vector", lambda v: v.reduce_max(nk2[:, 0:1], nred[:, 0:4], axis=AX.X),
           reads=["nred"], writes=["nk2"])

    load_w(wv_d)

    def evac_v(tt, oc, psb, psname):
        sl = (tt * 2 + oc) % 3
        st = kv_st[:, sl * 512: sl * 512 + 512]
        stn = f"kv{sl}"
        p.emit("vector", lambda v, st=st, psb=psb: v.tensor_copy(st, psb[:, 0:512]),
               reads=[psname], writes=[stn])
        p.emit("sync", lambda s, st=st, tt=tt, oc=oc: s.dma_start(
            out=bass.AP(vcc.tensor, (tt * 128) * D + oc * 512, [[D, 128], [1, 512]]),
            in_=st),
            reads=[stn], writes=[f"vcc{tt}_{oc}"], dma=True)
    mk.proj_tm(xln, "xln", w_sb, "w_sb", T, D, ps[0:4], psn[0:4], evac_v)

    p.barrier_from("gpsimd", [("syncD", p.cnt["syncD"])])
    p.emit("gpsimd", lambda g: g.dma_start(out=nkl_d[:], in_=nk2[0:1, 0:1]),
           reads=["nk2"], writes=["nkl"], dma=True)
    p.emit("gpsimd", lambda g: g.collective_compute(
        "AllGather", ALU.bypass, replica_groups=GROUPS, ins=[kcc[:]], outs=[kag[:]]),
        reads=["kcc"], writes=["kag"], coll=True)
    p.emit("gpsimd", lambda g: g.collective_compute(
        "AllGather", ALU.bypass, replica_groups=GROUPS, ins=[vcc[:]], outs=[vag[:]]),
        reads=["vcc"], writes=["vag"], coll=True)
    p.emit("gpsimd", lambda g: g.collective_compute(
        "AllReduce", ALU.max, replica_groups=GROUPS, ins=[nkl_d[:]], outs=[nkr_d[:]]),
        reads=["nkl"], writes=["nkr"], coll=True)
    p.emit("gpsimd", lambda g: g.dma_start(out=nred[0:1, 0:1], in_=nkr_d[:]),
           reads=["nkr", "nred"], writes=["nred"], dma=True)
    p.emit("tensor", lambda t: t.matmul(
        ps[6][:, 0:1], mk.ones_f[0:1, :], nred[0:1, 0:1], start=True, stop=True),
        reads=["nred", "ones_f"], writes=[psn[6]])
    p.emit("vector", lambda v: v.tensor_tensor(Bcol[:, 0:1], nq2[:, 0:1],
                                               ps[6][:, 0:1], op=ALU.mult),
           reads=["nq2", psn[6]], writes=["Bcol"])
    p.emit("scalar", lambda s: s.activation(Bcol[:, 0:1], Bcol[:, 0:1], AF.Sqrt),
           reads=["Bcol"], writes=["Bcol"])

    attn_core(mk, q_fm, "q_fm", kag, vag, x_sb, "x", ps, psn,
              ktile, vtile, ptile, stf, tsb, Bcol, outn, rcol, T)

    p.emit("sync", lambda s: s.dma_start(out=_rearr_fm(x_out), in_=x_sb[:]),
           reads=["x"], writes=["xout"], dma=True)
    p.barrier_from("gpsimd", [("syncD", p.cnt["syncD"])])


def attn_core(mk, q_fm, qname, kag, vag, x_sb, xname, ps, psn,
              ktile, vtile, ptile, stf, tsb, Bcol, outn, rcol, T):
    p = mk.p
    for group in ATT_GROUPS:
        pmax = group[-1]
        p0 = group[0]
        nkt = 4 * (pmax + 1)
        pend = [None]
        for kt in range(nkt + 1):
            buf = kt % 3
            if kt < nkt:
                m_k, p_k = kt % 4, kt // 4
                koff = (1024 * m_k) * T + 128 * p_k
                kdst = ktile[:, buf * DC * 128: (buf + 1) * DC * 128]
                p.emit("sync", lambda s, koff=koff, kdst=kdst: s.dma_start(
                    out=kdst,
                    in_=bass.AP(kag.tensor, koff, [[T, 128], [128 * T, DC], [1, 128]])),
                    reads=["kag"], writes=[f"ktile{buf}"], dma=True)
                voff = (2048 * m_k + 128 * p_k) * D
                vdst = vtile[:, buf * D: (buf + 1) * D]
                p.emit("sync", lambda s, voff=voff, vdst=vdst: s.dma_start(
                    out=vdst, in_=bass.AP(vag.tensor, voff, [[D, 128], [1, D]])),
                    reads=["vag"], writes=[f"vtile{buf}"], dma=True)
            if kt == 0:
                continue
            ck = kt - 1
            cb = ck % 3
            sbk = 4 + cb
            m_k, p_k = ck % 4, ck // 4
            act = [g for g in group if 4 * (g + 1) > ck]
            a0 = act[0]
            nq = 128 * len(act)
            kt_sb = ktile[:, cb * DC * 128: (cb + 1) * DC * 128]
            vt_sb = vtile[:, cb * D: (cb + 1) * D]
            pt_sb = ptile[:, cb * 256: cb * 256 + nq]
            st_sb = stf[:, cb * 256: cb * 256 + nq]
            with p.batch("tensor"):
                for dc in range(DC):
                    lhs = kt_sb[:, dc * 128: dc * 128 + 128]
                    rhs = q_fm[:, dc * T + 128 * a0: dc * T + 128 * a0 + nq]
                    p.emit("tensor", lambda t, lhs=lhs, rhs=rhs, dc=dc, nq=nq, sbk=sbk:
                           t.matmul(ps[sbk][:, 0:nq], lhs, rhs,
                                    start=(dc == 0), stop=(dc == DC - 1)),
                           reads=[f"ktile{cb}", qname], writes=[psn[sbk]])
            if pend[0] is not None:
                pend[0]()
            dq = ck // 4
            if dq in act:
                col0 = (dq - a0) * 128
                sl_out = st_sb[:, col0:col0 + 128]
                src = ps[sbk][:, col0:col0 + 128]
                u0 = 512 * dq - 128 * ck + 384
                p.emit("vector", lambda v, sl_out=sl_out, src=src, u0=u0:
                       v.scalar_tensor_tensor(
                           sl_out, src, Bcol[:, 0:1], tsb[:, u0:u0 + 128],
                           op0=ALU.subtract, op1=ALU.add),
                       reads=[psn[sbk], "Bcol", "tsb"], writes=[f"stf{cb}"])
            nond = [g for g in act if g != dq]
            runs = []
            for g in nond:
                if runs and runs[-1][-1] == g - 1:
                    runs[-1].append(g)
                else:
                    runs.append([g])
            for run in runs:
                col0 = (run[0] - a0) * 128
                w = 128 * len(run)
                sl_out = st_sb[:, col0:col0 + w]
                src = ps[sbk][:, col0:col0 + w]
                p.emit("vector", lambda v, sl_out=sl_out, src=src:
                       v.tensor_scalar(sl_out, src, Bcol[:, 0:1], None, op0=ALU.subtract),
                       reads=[psn[sbk], "Bcol"], writes=[f"stf{cb}"])
            p.emit("scalar", lambda s, pt_sb=pt_sb, st_sb=st_sb: s.activation(
                pt_sb, st_sb, AF.Exp),
                reads=[f"stf{cb}"], writes=[f"ptile{cb}"])

            def mkpv(ck=ck, cb=cb, act=tuple(act), a0=a0, vt_sb=vt_sb, group=group,
                     p0=p0, pmax=pmax):
                with p.batch("tensor"):
                    for g in act:
                        col0 = (g - a0) * 128
                        lhs = ptile[:, cb * 256 + col0: cb * 256 + col0 + 128]
                        first = (ck == 0)
                        last = (ck == 4 * g + 3)
                        p.emit("tensor", lambda t, lhs=lhs, g=g, first=first, last=last:
                               t.matmul(
                            ps[7][:, g - p0: g - p0 + 1], lhs, mk.ones_b[:, 0:1],
                            start=(first and g == a0), stop=(last and g == pmax)),
                            reads=[f"ptile{cb}", "ones_b"], writes=[psn[7]])
                        gi = group.index(g)
                        for half in range(2):
                            p.emit("tensor", lambda t, lhs=lhs, gi=gi, half=half,
                                   first=first, last=last: t.matmul(
                                       ps[2 * gi + half][:, 0:512], lhs,
                                       vt_sb[:, half * 512: half * 512 + 512],
                                       start=first, stop=last),
                                   reads=[f"ptile{cb}", f"vtile{cb}"],
                                   writes=[psn[2 * gi + half]])
            pend[0] = mkpv
        pend[0]()
        pend[0] = None
        for gi, g in enumerate(group):
            p.emit("vector", lambda v, gi=gi, g=g, p0=p0: v.reciprocal(
                rcol[:, gi:gi + 1], ps[7][:, g - p0: g - p0 + 1]),
                reads=[psn[7]], writes=["rcol"])
            for half in range(2):
                dst = outn[:, half * 512: half * 512 + 512]
                p.emit("vector", lambda v, dst=dst, gi=gi, half=half: v.tensor_scalar(
                    dst, ps[2 * gi + half][:, 0:512], rcol[:, gi:gi + 1], None,
                    op0=ALU.mult),
                    reads=[psn[2 * gi + half], "rcol"], writes=[f"outn{half}"])
            for dc in range(DC):
                tb_ = 4 + dc % 3
                src = outn[:, dc * 128: dc * 128 + 128]
                p.emit("tensor", lambda t, src=src, tb_=tb_: t.transpose(
                    ps[tb_][:, 0:128], src, mk.id_f[:]),
                    reads=[f"outn{dc // 4}", "id_f"], writes=[psn[tb_]])
                xs = x_sb[:, dc * T + 128 * g: dc * T + 128 * g + 128]
                p.emit("vector", lambda v, xs=xs, tb_=tb_: v.tensor_add(
                    xs, xs, ps[tb_][:, 0:128]),
                    reads=[psn[tb_], xname], writes=[xname])



_PROGRAMS = {}


def get_program(name, builder):
    if name not in _PROGRAMS:
        nc = bass.Bass("TRN2", target_bir_lowering=False)
        mk = MK(nc)
        builder(mk)
        assert mk.p.sbuf_bytes <= 219 * 1024, f"sbuf over budget: {mk.p.sbuf_bytes}"
        mk.p.build()
        _PROGRAMS[name] = nc
    return _PROGRAMS[name]


def attn_a_masktab(c):
    m = c % 4
    u = np.arange(640)[None, :]
    kk = np.arange(128)[:, None]
    return np.where(u - kk >= 384 - 128 * m, 0.0, NEG).astype(np.float32)


def run_attn_a(xs, wq, wk, wv):
    nc = get_program("attn_a", build_attn_a)
    in_maps = []
    for c in range(NCORE):
        in_maps.append({
            "x_in": xs[c],
            "wq": wq, "wk": wk, "wv": wv,
            "mtab": attn_a_masktab(c),
        })
    res = run_bass_kernel_spmd(nc, in_maps, core_ids=list(range(NCORE)))
    return [res.results[c]["x_out"] for c in range(NCORE)]


def _w_ap(d_ap, rows, cols, row0=0, col0=0, rstride=None):
    C = d_ap.shape[-1] if rstride is None else rstride
    nrc = rows // 128
    return bass.AP(d_ap.tensor, row0 * C + col0, [[C, 128], [128 * C, nrc], [1, cols]])


class LNStream:

    def __init__(self, mk, T):
        p = mk.p
        self.mk = mk
        self.T = T
        if not hasattr(mk, "ln_scr"):
            mk.ln_scr = p.sbuf("ln_scr", [128, 1024], F32, persistent=True)
            mk.ln_A = p.sbuf("ln_A", [128, 2048], F32, persistent=True)
            mk.ln_M = p.sbuf("ln_M", [128, 2048], F32, persistent=True)
            mk.ln_t = p.sbuf("ln_t", [128, 1024], F32, persistent=True)
        if not hasattr(mk, "xt"):
            mk.xt = p.sbuf("xt", [128, 2048], F32, persistent=True)

    def run(self, x_dram, xln, lnname, ps_sum, ps_sq, psn_sum, psn_sq,
            rstride=None, col0=0, srcfn=None):
        mk, p, T = self.mk, self.mk.p, self.T
        rs = rstride if rstride is not None else T
        scr, At, Mt, tmp, xt = mk.ln_scr, mk.ln_A, mk.ln_M, mk.ln_t, mk.xt

        def xload(dc, tch, sl):
            dst = xt[:, sl * 512: sl * 512 + 512]
            if srcfn is not None:
                sap = srcfn(dc, tch)
            else:
                sap = bass.AP(x_dram.tensor, (dc * 128) * rs + col0 + tch * 512,
                              [[rs, 128], [1, 512]])
            p.emit("sync", lambda s, dst=dst, sap=sap: s.dma_start(out=dst, in_=sap),
                   writes=[f"xt{sl}"], dma=True)
            return dst

        sqslots = [(scr[:, 0:512], "ln_scr0"), (scr[:, 512:1024], "ln_scr1"),
                   (tmp[:, 0:512], "ln_ta"), (tmp[:, 512:1024], "ln_tb")]
        for tch in range(T // 512):
            for dc in range(DC):
                sl = dc % 4
                xs = xload(dc, tch, sl)
                p.emit("tensor", lambda t, xs=xs, dc=dc: t.matmul(
                    ps_sum[:, 0:512], mk.ones_f[:], xs, start=(dc == 0), stop=(dc == DC - 1)),
                    reads=[f"xt{sl}", "ones_f"], writes=[psn_sum])
                sq, sqn = sqslots[dc % 4]
                p.emit("scalar", lambda s, xs=xs, sq=sq: s.activation(sq, xs, AF.Square),
                       reads=[f"xt{sl}"], writes=[sqn])
                p.emit("tensor", lambda t, sq=sq, dc=dc: t.matmul(
                    ps_sq[:, 0:512], mk.ones_f[:], sq, start=(dc == 0), stop=(dc == DC - 1)),
                    reads=[sqn, "ones_f"], writes=[psn_sq])
            Ms = Mt[:, tch * 512: tch * 512 + 512]
            As = At[:, tch * 512: tch * 512 + 512]
            t0 = tmp[:, 0:512]
            t2 = tmp[:, 512:1024]
            p.emit("vector", lambda v, Ms=Ms: v.tensor_scalar(
                Ms, ps_sum[:, 0:512], 1.0 / D, None, op0=ALU.mult),
                reads=[psn_sum], writes=["ln_M"])
            p.emit("vector", lambda v, t0=t0: v.tensor_scalar(
                t0, ps_sq[:, 0:512], 1.0 / D, None, op0=ALU.mult),
                reads=[psn_sq], writes=["ln_ta"])
            p.emit("vector", lambda v, Ms=Ms, t2=t2: v.tensor_tensor(
                t2, Ms, Ms, op=ALU.mult), reads=["ln_M"], writes=["ln_tb"])
            p.emit("vector", lambda v, t0=t0, t2=t2: v.tensor_sub(t0, t0, t2),
                   reads=["ln_ta", "ln_tb"], writes=["ln_ta"])
            eps = mk.constcol(1e-5)
            p.emit("scalar", lambda s, t0=t0, eps=eps: s.activation(
                t0, t0, AF.Sqrt, bias=eps[:, 0:1]),
                reads=["ln_ta", "cc_1e-05"], writes=["ln_ta"])
            p.emit("vector", lambda v, As=As, t0=t0: v.reciprocal(As, t0),
                   reads=["ln_ta"], writes=["ln_A"])
        for tch in range(T // 512):
            Ms = Mt[:, tch * 512: tch * 512 + 512]
            As = At[:, tch * 512: tch * 512 + 512]
            for dc in range(DC):
                sl = dc % 4
                xs = xload(dc, tch, sl)
                ys = xln[:, dc * T + tch * 512: dc * T + tch * 512 + 512]
                ssl = dc % 2
                t1 = scr[:, ssl * 512:ssl * 512 + 512]
                p.emit("vector", lambda v, xs=xs, Ms=Ms, t1=t1: v.tensor_sub(t1, xs, Ms),
                       reads=[f"xt{sl}", "ln_M"], writes=[f"ln_scr{ssl}"])
                p.emit("vector", lambda v, ys=ys, t1=t1, As=As: v.tensor_tensor(
                    ys, t1, As, op=ALU.mult),
                    reads=[f"ln_scr{ssl}", "ln_A"], writes=[lnname])


def build_ffn0(mk, xin=None, xout=None, wsh=None, wrec=None):
    nc, p = mk.nc, mk.p
    def win(name, shape, dtype=F32):
        if wsh is not None and name in wsh:
            return wsh[name]
        apx = p.dram(name, shape, dtype, kind="ExternalInput").ap()
        if wrec is not None:
            wrec[name] = apx
        return apx

    T = TPC
    E, NI = 16, 1024
    x_in = xin if xin is not None else p.dram("x_in", [D, T], F32, kind="ExternalInput").ap()
    gwT = win("gwT", [D, E], F32)
    ebias = win("ebias", [1, E], F32)
    wgT = win("wgT", [E, D, NI], F32)
    wuT = win("wuT", [E, D, NI], F32)
    wdT = win("wdT", [E, NI, D], F32)
    sgT = win("sgT", [D, NI], F32)
    suT = win("suT", [D, NI], F32)
    sdT = win("sdT", [NI, D], F32)
    x_out = xout if xout is not None else p.dram("x_out", [D, T], F32, kind="ExternalOutput").ap()

    xln = p.sbuf("xln", [128, DC * T], BF16)
    y_sb = p.sbuf("y_sb", [128, NP * 1024], F32)
    wg_sb = p.sbuf("wg_sb", [128, DC * NI], BF16)
    wu_sb = p.sbuf("wu_sb", [128, DC * NI], BF16)
    wd_sb = p.sbuf("wd_sb", [128, DC * D], BF16)
    h_sb = p.sbuf("h_sb", [128, DC * 512], BF16)
    sg_sb = p.sbuf("sg_sb", [128, 4 * 512], F32)
    gw_sb = p.sbuf("gw_sb", [128, DC * E], BF16)
    eb_sb = p.sbuf("eb_sb", [1, E], F32)
    bb_sb = p.sbuf("bb_sb", [128, E], F32)
    sc_tm = p.sbuf("sc_tm", [128, NP * E], F32)
    comb = p.sbuf("comb", [128, NP * E], F32)
    r1 = p.sbuf("r1", [128, 2 * E], F32)
    r2 = p.sbuf("r2", [128, E], F32)
    rden = p.sbuf("rden", [128, 2], F32)

    ps = [p.psum(f"ps{i}", [128, 512]) for i in range(8)]
    psn = [f"ps{i}" for i in range(8)]

    mk.setup_consts()
    ln = LNStream(mk, T)
    ln.run(x_in, xln, "xln", ps[0], ps[1], psn[0], psn[1])

    p.emit("gpsimd", lambda s: s.dma_start(
        out=gw_sb[:], in_=bass.AP(gwT.tensor, 0, [[E, 128], [128 * E, DC], [1, E]])),
        writes=["gw_sb"], dma=True)
    p.emit("sync", lambda s: s.dma_start(out=eb_sb[:], in_=ebias[:]),
           writes=["eb_sb"], dma=True)
    p.emit("tensor", lambda t: t.matmul(
        ps[3][:, 0:E], mk.ones_f[0:1, :], eb_sb[0:1, 0:E], start=True, stop=True),
        reads=["eb_sb", "ones_f"], writes=[psn[3]])
    p.emit("vector", lambda v: v.tensor_copy(bb_sb[:], ps[3][:, 0:E]),
           reads=[psn[3]], writes=["bb_sb"])
    for tt in range(NP):
        sl2 = tt % 2
        rb, rbn = ps[2 + sl2], psn[2 + sl2]
        r1s = r1[:, sl2 * E: sl2 * E + E]
        r2s = r2[:, sl2 * 8: sl2 * 8 + 8]
        rds = rden[:, sl2:sl2 + 1]
        with p.batch("tensor"):
            for dc in range(DC):
                lhs = xln[:, dc * T + tt * 128: dc * T + tt * 128 + 128]
                rhs = gw_sb[:, dc * E: dc * E + E]
                p.emit("tensor", lambda t, lhs=lhs, rhs=rhs, dc=dc, rb=rb: t.matmul(
                    rb[:, 0:E], lhs, rhs, start=(dc == 0), stop=(dc == DC - 1)),
                    reads=["xln", "gw_sb"], writes=[rbn])
        sco = sc_tm[:, tt * E: tt * E + E]
        p.emit("scalar", lambda s, sco=sco, rb=rb: s.activation(
            sco, rb[:, 0:E], AF.Sigmoid),
            reads=[rbn], writes=["sc_tm"])
        p.emit("vector", lambda v, sco=sco, r1s=r1s: v.tensor_add(r1s, sco, bb_sb[:]),
               reads=["sc_tm", "bb_sb"], writes=[f"r1{sl2}"])
        p.emit("vector", lambda v, r1s=r1s, r2s=r2s: v.max(out=r2s, in_=r1s),
               reads=[f"r1{sl2}"], writes=[f"r2{sl2}"])
        p.emit("vector", lambda v, r1s=r1s, r2s=r2s: v.match_replace(
            out=r1s, in_to_replace=r2s,
            in_values=r1s, imm_value=-1e30),
            reads=[f"r1{sl2}", f"r2{sl2}"], writes=[f"r1{sl2}"])
        cmb = comb[:, tt * E: tt * E + E]
        p.emit("vector", lambda v, cmb=cmb, r1s=r1s: v.tensor_scalar(
            cmb, r1s, -1e30, None, op0=ALU.is_equal),
            reads=[f"r1{sl2}"], writes=["comb"])
        p.emit("vector", lambda v, cmb=cmb, sco=sco: v.tensor_tensor(
            cmb, cmb, sco, op=ALU.mult), reads=["comb", "sc_tm"], writes=["comb"])
        p.emit("vector", lambda v, cmb=cmb, rds=rds: v.reduce_sum(
            rds, cmb, axis=AX.X), reads=["comb"], writes=[f"rd{sl2}"])
        p.emit("vector", lambda v, rds=rds: v.reciprocal(rds, rds),
               reads=[f"rd{sl2}"], writes=[f"rd{sl2}"])
        p.emit("vector", lambda v, cmb=cmb, rds=rds: v.tensor_scalar(
            cmb, cmb, rds, 2.826, op0=ALU.mult, op1=ALU.mult),
            reads=["comb", f"rd{sl2}"], writes=["comb"])

    if DBG_FFN0:
        d1 = p.dram("dbg_comb", [128, NP * E], F32, kind="ExternalOutput").ap()
        p.emit("gpsimd", lambda g, d1=d1: g.dma_start(out=d1[:], in_=comb[:]),
               reads=["comb"], writes=["dcomb"], dma=True)
        d2 = p.dram("dbg_xln", [128, DC * T], F32, kind="ExternalOutput").ap()
        p.emit("gpsimd", lambda g, d2=d2: g.dma_start(out=d2[:], in_=xln[:]),
               reads=["xln"], writes=["dxln"], dma=True)
        d3 = p.dram("dbg_sc", [128, NP * E], F32, kind="ExternalOutput").ap()
        p.emit("gpsimd", lambda g, d3=d3: g.dma_start(out=d3[:], in_=sc_tm[:]),
               reads=["sc_tm"], writes=["dsc"], dma=True)
    for j in range(4):
        p.emit("vector", lambda v, j=j: v.memset(y_sb[:, j * 4096:(j + 1) * 4096], 0.0),
               writes=["y_sb"])

    pre = wsh is not None and "wgb" in wsh
    if pre:
        wgb, wub, wdb = wsh["wgb"], wsh["wub"], wsh["wdb"]
        sgb, sub, sdb = wsh["sgb"], wsh["sub"], wsh["sdb"]
    else:
        wgb = p.dram("wgb", [E * D, NI], BF16).ap()
        wub = p.dram("wub", [E * D, NI], BF16).ap()
        wdb = p.dram("wdb", [E * NI, D], BF16).ap()
        sgb = p.dram("sgb", [D, NI], BF16).ap()
        sub = p.dram("sub", [D, NI], BF16).ap()
        sdb = p.dram("sdb", [NI, D], BF16).ap()
        if wrec is not None:
            wrec.update(wgb=wgb, wub=wub, wdb=wdb, sgb=sgb, sub=sub, sdb=sdb)
    _ncast = [0]

    def ensure_cast(upto):
        while _ncast[0] < min(upto, 17):
            k = _ncast[0]
            _ncast[0] += 1
            if pre:
                continue
            if k < E:
                trio = ((wgb, wgT, k * D, NI), (wub, wuT, k * D, NI),
                        (wdb, wdT, k * NI, D))
            else:
                trio = ((sgb, sgT, 0, NI), (sub, suT, 0, NI), (sdb, sdT, 0, D))
            for i, (bdst, fsrc, r0, C) in enumerate(trio):
                rows = D if C == NI else NI
                p.emit("gpsimd", lambda g, bdst=bdst, fsrc=fsrc, r0=r0, C=C,
                       rows=rows: g.dma_start(
                           out=bass.AP(bdst.tensor, r0 * C, [[C, rows], [1, C]]),
                           in_=bass.AP(fsrc.tensor, r0 * C, [[C, rows], [1, C]])),
                    writes=[f"wcast{k}_{i}"], dma=True)

    def expert(e, wg_src, wu_src, wd_src, use_comb, crd):
        p.emit("sync", lambda g, w=wg_src: g.dma_start(out=wg_sb[:], in_=w),
               reads=[f"wcast{crd}_0"], writes=["wg_sb"], dma=True)
        p.emit("sync", lambda g, w=wu_src: g.dma_start(out=wu_sb[:], in_=w),
               reads=[f"wcast{crd}_1"], writes=["wu_sb"], dma=True)
        p.emit("sync", lambda g, w=wd_src: g.dma_start(out=wd_sb[:], in_=w),
               reads=[f"wcast{crd}_2"], writes=["wd_sb"], dma=True)
        for tb in range(4):
            for ic in range(DC):
                r = ic % 3
                pg, pgn = ps[2 * r], psn[2 * r]
                pu, pun = ps[2 * r + 1], psn[2 * r + 1]
                with p.batch("tensor"):
                    for dc in range(DC):
                        rhs = xln[:, dc * T + tb * 512: dc * T + tb * 512 + 512]
                        lg = wg_sb[:, dc * NI + ic * 128: dc * NI + ic * 128 + 128]
                        p.emit("tensor", lambda t, lg=lg, rhs=rhs, dc=dc, pg=pg: t.matmul(
                            pg[:, 0:512], lg, rhs, start=(dc == 0), stop=(dc == DC - 1)),
                            reads=["wg_sb", "xln"], writes=[pgn])
                    for dc in range(DC):
                        rhs = xln[:, dc * T + tb * 512: dc * T + tb * 512 + 512]
                        lu = wu_sb[:, dc * NI + ic * 128: dc * NI + ic * 128 + 128]
                        p.emit("tensor", lambda t, lu=lu, rhs=rhs, dc=dc, pu=pu: t.matmul(
                            pu[:, 0:512], lu, rhs, start=(dc == 0), stop=(dc == DC - 1)),
                            reads=["wu_sb", "xln"], writes=[pun])
                sl = ic % 4
                sg = sg_sb[:, sl * 512: sl * 512 + 512]
                p.emit("scalar", lambda s, sg=sg, pg=pg: s.activation(
                    sg, pg[:, 0:512], AF.Silu),
                    reads=[pgn], writes=[f"sg{sl}"])
                hd = h_sb[:, ic * 512: ic * 512 + 512]
                p.emit("vector", lambda v, hd=hd, sg=sg, pu=pu: v.tensor_tensor(
                    hd, pu[:, 0:512], sg, op=ALU.mult),
                    reads=[pun, f"sg{sl}"], writes=["h_sb"])
            dbanks = [6, 7, 0, 1]
            for j in range(4):
                tt = tb * 4 + j
                for half in range(2):
                    bi = dbanks[(j * 2 + half) % 4]
                    pb, pbn = ps[bi], psn[bi]
                    with p.batch("tensor"):
                        for ic in range(DC):
                            lhs = h_sb[:, ic * 512 + j * 128: ic * 512 + j * 128 + 128]
                            rhs = wd_sb[:, ic * D + half * 512: ic * D + half * 512 + 512]
                            p.emit("tensor", lambda t, lhs=lhs, rhs=rhs, ic=ic, pb=pb: t.matmul(
                                pb[:, 0:512], lhs, rhs, start=(ic == 0), stop=(ic == DC - 1)),
                                reads=["h_sb", "wd_sb"], writes=[pbn])
                    ysl = y_sb[:, tt * 1024 + half * 512: tt * 1024 + half * 512 + 512]
                    if use_comb:
                        ccol = comb[:, tt * E + e: tt * E + e + 1]
                        p.emit("vector", lambda v, ysl=ysl, ccol=ccol, pb=pb:
                               v.scalar_tensor_tensor(ysl, pb[:, 0:512], ccol, ysl,
                                                      op0=ALU.mult, op1=ALU.add),
                               reads=[pbn, "comb", "y_sb"], writes=["y_sb"])
                    else:
                        p.emit("vector", lambda v, ysl=ysl, pb=pb: v.tensor_add(
                            ysl, ysl, pb[:, 0:512]),
                            reads=[pbn, "y_sb"], writes=["y_sb"])

    for e in range(E):
        ensure_cast(e + 3)
        expert(e,
               _w_ap(wgb, D, NI, row0=e * D, rstride=NI),
               _w_ap(wub, D, NI, row0=e * D, rstride=NI),
               _w_ap(wdb, NI, D, row0=e * NI, rstride=D),
               use_comb=True, crd=e)
    ensure_cast(17)
    expert(0, _w_ap(sgb, D, NI), _w_ap(sub, D, NI), _w_ap(sdb, NI, D),
           use_comb=False, crd=16)

    xt = mk.xt
    for tch in range(4):
        for dc in range(DC):
            sl = dc % 2
            xs = xt[:, sl * 512: sl * 512 + 512]
            p.emit("sync", lambda s, dc=dc, tch=tch, xs=xs: s.dma_start(
                out=xs, in_=bass.AP(x_in.tensor, (dc * 128) * T + tch * 512,
                                    [[T, 128], [1, 512]])),
                writes=[f"xt{sl}"], dma=True)
            for j in range(4):
                tt = tch * 4 + j
                tb_ = 2 + j
                src = y_sb[:, tt * 1024 + dc * 128: tt * 1024 + dc * 128 + 128]
                p.emit("tensor", lambda t, src=src, tb_=tb_: t.transpose(
                    ps[tb_][:, 0:128], src, mk.id_f[:]),
                    reads=["y_sb", "id_f"], writes=[psn[tb_]])
                p.emit("vector", lambda v, xs=xs, j=j, tb_=tb_: v.tensor_add(
                    xs[:, j * 128: j * 128 + 128], xs[:, j * 128: j * 128 + 128],
                    ps[tb_][:, 0:128]),
                    reads=[psn[tb_], f"xt{sl}"], writes=[f"xt{sl}"])
            p.emit("sync", lambda s, dc=dc, tch=tch, xs=xs: s.dma_start(
                out=bass.AP(x_out.tensor, (dc * 128) * T + tch * 512,
                            [[T, 128], [1, 512]]), in_=xs),
                reads=[f"xt{sl}"], writes=[f"xo{dc}_{tch}"], dma=True)
    p.barrier_from("gpsimd", [("syncD", p.cnt["syncD"])])


def run_ffn0(xs, gate_w, ebias, w_gate, w_up, w_down, sg, su, sd):
    nc = get_program("ffn0", build_ffn0)
    gwT = np.ascontiguousarray(gate_w.T)
    wgT = np.ascontiguousarray(np.transpose(w_gate, (0, 2, 1)))
    wuT = np.ascontiguousarray(np.transpose(w_up, (0, 2, 1)))
    wdT = np.ascontiguousarray(np.transpose(w_down, (0, 2, 1)))
    sgT = np.ascontiguousarray(sg.T)
    suT = np.ascontiguousarray(su.T)
    sdT = np.ascontiguousarray(sd.T)
    in_maps = []
    for c in range(NCORE):
        in_maps.append({
            "x_in": xs[c], "gwT": gwT, "ebias": ebias.reshape(1, -1),
            "wgT": wgT, "wuT": wuT, "wdT": wdT,
            "sgT": sgT, "suT": suT, "sdT": sdT,
        })
    res = run_bass_kernel_spmd(nc, in_maps, core_ids=list(range(NCORE)))
    return [res.results[c]["x_out"] for c in range(NCORE)]



def build_attn_b(mk, xin=None, xout=None, wsh=None, wrec=None):
    nc, p = mk.nc, mk.p
    def win(name, shape, dtype=F32):
        if wsh is not None and name in wsh:
            return wsh[name]
        apx = p.dram(name, shape, dtype, kind="ExternalInput").ap()
        if wrec is not None:
            wrec[name] = apx
        return apx

    T = TPC
    NH, DK = 16, 64
    TW = 2688
    x_in = xin if xin is not None else p.dram("x_in", [D, T], F32, kind="ExternalInput").ap()
    wq_d = win("wq", [D, D], F32)
    wk_d = win("wk", [D, D], F32)
    wv_d = win("wv", [D, D], F32)
    wo_d = win("wo", [D, D], F32)
    bq_d = win("bq", [128, DC], F32)
    bk_d = win("bk", [128, DC], F32)
    bvr_d = win("bvr", [1, D], F32)
    bo_d = win("bo", [128, DC], F32)
    rtab_d = win("rtab", [NH, 128, TW], F32)
    ctab_d = win("ctab", [1, NH], F32)
    x_out = xout if xout is not None else p.dram("x_out", [D, T], F32, kind="ExternalOutput").ap()
    kcc = p.dram("kcc", [D, T], BF16).ap()
    vcc = p.dram("vcc", [T, D], BF16).ap()
    kag = p.dram("kag", [4 * D, T], BF16).ap()
    vag = p.dram("vag", [4 * T, D], BF16).ap()
    nkl_d = p.dram("nkl", [1, 1], F32).ap()
    nkr_d = p.dram("nkr", [1, 1], F32).ap()

    xln = p.sbuf("xln", [128, DC * T], BF16)
    q_fm = p.sbuf("q_fm", [128, DC * T], BF16)
    o_fm = p.sbuf("o_fm", [128, DC * T], BF16)
    w_sb = p.sbuf("w_sb", [128, DC * D], BF16)
    kh_sb = p.sbuf("kh_sb", [128, S], BF16)
    vh_sb = p.sbuf("vh_sb", [128, 64 * 65], BF16)
    tsb = p.sbuf("tsb", [128, TW], F32)
    kv_st = p.sbuf("kv_st", [128, 3 * 512], BF16)
    nsc = p.sbuf("nsc", [128, 1024], F32)
    bvbc = p.sbuf("bvbc", [128, D], F32)
    ptile = p.sbuf("ptile", [128, 3 * 512], BF16)
    stf = p.sbuf("stf", [128, 3 * 512], F32)
    rrow = p.sbuf("rrow", [128, 512], F32)
    rbc = p.sbuf("rbc", [128, 512], F32)
    nq2 = p.sbuf("nq2", [128, 1], F32)
    nk2 = p.sbuf("nk2", [128, 1], F32)
    nred = p.sbuf("nred", [128, 4], F32)
    Bqk = p.sbuf("Bqk", [128, 1], F32)
    BMn = p.sbuf("BMn", [128, 1], F32)
    BMf = p.sbuf("BMf", [128, 1], F32)
    ctb = p.sbuf("ctb", [128, NH], F32)
    bia = p.sbuf("bia", [128, DC], F32)

    ps = [p.psum(f"ps{i}", [128, 512]) for i in range(8)]
    psn = [f"ps{i}" for i in range(8)]

    mk.setup_consts()
    ln = LNStream(mk, T)
    ln.run(x_in, xln, "xln", ps[0], ps[1], psn[0], psn[1])

    def load_w(dram_ap):
        p.emit("gpsimd", lambda g, d=dram_ap: g.dma_start(
            out=w_sb[:], in_=bass.AP(d.tensor, 0, [[D, 128], [128 * D, DC], [1, D]])),
            writes=["w_sb"], dma=True)

    def load_bias(dram_ap):
        p.emit("sync", lambda s, d=dram_ap: s.dma_start(out=bia[:], in_=d[:]),
               writes=["bia"], dma=True)

    load_w(wq_d)
    load_bias(bq_d)

    def evac_q(oc, tch, psb, psname):
        dst = q_fm[:, oc * T + tch * 512: oc * T + tch * 512 + 512]
        p.emit("vector", lambda v, dst=dst, psb=psb, oc=oc: v.tensor_scalar(
            dst, psb[:, 0:512], bia[:, oc:oc + 1], None, op0=ALU.add),
            reads=[psname, "bia"], writes=["q_fm"])
    mk.proj_fm(xln, "xln", w_sb, "w_sb", T, D, ps[0:4], psn[0:4], evac_q)

    for tch in range(4):
        for dc in range(DC):
            sl = dc % 2
            sq = nsc[:, sl * 512: sl * 512 + 512]
            src = q_fm[:, dc * T + tch * 512: dc * T + tch * 512 + 512]
            p.emit("scalar", lambda s, sq=sq, src=src: s.activation(sq, src, AF.Square),
                   reads=["q_fm"], writes=[f"nsc{sl}"])
            p.emit("tensor", lambda t, sq=sq, dc=dc: t.matmul(
                ps[4][:, 0:512], mk.ones_f[:], sq, start=(dc == 0), stop=(dc == DC - 1)),
                reads=[f"nsc{sl}", "ones_f"], writes=[psn[4]])
        p.emit("vector", lambda v, tch=tch: v.reduce_max(
            nred[:, tch:tch + 1], ps[4][:, 0:512], axis=AX.X),
            reads=[psn[4]], writes=["nred"])
    p.emit("vector", lambda v: v.reduce_max(nq2[:, 0:1], nred[:, 0:4], axis=AX.X),
           reads=["nred"], writes=["nq2"])

    load_w(wk_d)
    load_bias(bk_d)

    def evac_k(oc, tch, psb, psname):
        sl = (oc + tch) % 3
        st = kv_st[:, sl * 512: sl * 512 + 512]
        stn = f"kv{sl}"
        p.emit("vector", lambda v, st=st, psb=psb, oc=oc: v.tensor_scalar(
            st, psb[:, 0:512], bia[:, oc:oc + 1], 0.125, op0=ALU.add, op1=ALU.mult),
            reads=[psname, "bia"], writes=[stn])
        p.emit("sync", lambda s, st=st, oc=oc, tch=tch: s.dma_start(
            out=bass.AP(kcc.tensor, (oc * 128) * T + tch * 512, [[T, 128], [1, 512]]),
            in_=st),
            reads=[stn], writes=[f"kcc{oc}_{tch}"], dma=True)
        ssl = oc % 2
        sq = nsc[:, ssl * 512: ssl * 512 + 512]
        p.emit("scalar", lambda s, sq=sq, st=st: s.activation(sq, st, AF.Square),
               reads=[stn], writes=[f"nsc{ssl}"])
        p.emit("tensor", lambda t, sq=sq, oc=oc: t.matmul(
            ps[5][:, 0:512], mk.ones_f[:], sq, start=(oc == 0), stop=(oc == DC - 1)),
            reads=[f"nsc{ssl}", "ones_f"], writes=[psn[5]])
        if oc == DC - 1:
            p.emit("vector", lambda v, tch=tch: v.reduce_max(
                nred[:, tch:tch + 1], ps[5][:, 0:512], axis=AX.X),
                reads=[psn[5]], writes=["nred"])
    mk.proj_fm(xln, "xln", w_sb, "w_sb", T, D, ps[0:4], psn[0:4], evac_k)
    p.emit("vector", lambda v: v.reduce_max(nk2[:, 0:1], nred[:, 0:4], axis=AX.X),
           reads=["nred"], writes=["nk2"])

    load_w(wv_d)
    p.emit("sync", lambda s: s.dma_start(out=rrow[0:1, 0:512], in_=bvr_d[0:1, 0:512]),
           writes=["bvr0"], dma=True)
    p.emit("sync", lambda s: s.dma_start(out=nsc[0:1, 0:512], in_=bvr_d[0:1, 512:1024]),
           writes=["bvr1"], dma=True)
    for half in range(2):
        src = rrow[0:1, 0:512] if half == 0 else nsc[0:1, 0:512]
        rname = f"bvr{half}"
        p.emit("tensor", lambda t, src=src: t.matmul(
            ps[6][:, 0:512], mk.ones_f[0:1, :], src, start=True, stop=True),
            reads=[rname, "ones_f"], writes=[psn[6]])
        p.emit("vector", lambda v, half=half: v.tensor_copy(
            bvbc[:, half * 512: half * 512 + 512], ps[6][:, 0:512]),
            reads=[psn[6]], writes=["bvbc"])

    def evac_v(tt, oc, psb, psname):
        sl = (tt * 2 + oc) % 3
        st = kv_st[:, sl * 512: sl * 512 + 512]
        stn = f"kv{sl}"
        p.emit("vector", lambda v, st=st, psb=psb, oc=oc: v.tensor_tensor(
            st, psb[:, 0:512], bvbc[:, oc * 512: oc * 512 + 512], op=ALU.add),
            reads=[psname, "bvbc"], writes=[stn])
        p.emit("sync", lambda s, st=st, tt=tt, oc=oc: s.dma_start(
            out=bass.AP(vcc.tensor, (tt * 128) * D + oc * 512, [[D, 128], [1, 512]]),
            in_=st),
            reads=[stn], writes=[f"vcc{tt}_{oc}"], dma=True)
    mk.proj_tm(xln, "xln", w_sb, "w_sb", T, D, ps[0:4], psn[0:4], evac_v)

    p.barrier_from("gpsimd", [("syncD", p.cnt["syncD"])])
    p.emit("gpsimd", lambda g: g.dma_start(out=nkl_d[:], in_=nk2[0:1, 0:1]),
           reads=["nk2"], writes=["nkl"], dma=True)
    p.emit("gpsimd", lambda g: g.collective_compute(
        "AllGather", ALU.bypass, replica_groups=GROUPS, ins=[kcc[:]], outs=[kag[:]]),
        reads=["kcc"], writes=["kag"], coll=True)
    p.emit("gpsimd", lambda g: g.collective_compute(
        "AllGather", ALU.bypass, replica_groups=GROUPS, ins=[vcc[:]], outs=[vag[:]]),
        reads=["vcc"], writes=["vag"], coll=True)
    p.emit("gpsimd", lambda g: g.collective_compute(
        "AllReduce", ALU.max, replica_groups=GROUPS, ins=[nkl_d[:]], outs=[nkr_d[:]]),
        reads=["nkl"], writes=["nkr"], coll=True)
    p.emit("gpsimd", lambda g: g.dma_start(out=nred[0:1, 0:1], in_=nkr_d[:]),
           reads=["nkr", "nred"], writes=["nred"], dma=True)
    p.emit("tensor", lambda t: t.matmul(
        ps[6][:, 0:1], mk.ones_f[0:1, :], nred[0:1, 0:1], start=True, stop=True),
        reads=["nred", "ones_f"], writes=[psn[6]])
    p.emit("vector", lambda v: v.tensor_tensor(Bqk[:, 0:1], nq2[:, 0:1],
                                               ps[6][:, 0:1], op=ALU.mult),
           reads=["nq2", psn[6]], writes=["Bqk"])
    p.emit("scalar", lambda s: s.activation(Bqk[:, 0:1], Bqk[:, 0:1], AF.Sqrt),
           reads=["Bqk"], writes=["Bqk"])
    p.emit("sync", lambda s: s.dma_start(out=nsc[0:1, 0:NH], in_=ctab_d[:]),
           reads=["nsc0"], writes=["ctabr"], dma=True)
    p.emit("tensor", lambda t: t.matmul(
        ps[6][:, 0:NH], mk.ones_f[0:1, :], nsc[0:1, 0:NH], start=True, stop=True),
        reads=["ctabr", "ones_f"], writes=[psn[6]])
    p.emit("vector", lambda v: v.tensor_copy(ctb[:], ps[6][:, 0:NH]),
           reads=[psn[6]], writes=["ctb"])

    p.emit("vector", lambda v: v.memset(
        bass.AP(vh_sb, 64, [[64 * 65, 128], [65, 64], [1, 1]]), 1.0),
        writes=["vh1"])

    sti = [0]
    for h in range(NH):
        hb = 64 * (h % 2)
        hdc = h // 2
        for mk_ in range(4):
            p.emit("sync", lambda s, h=h, hb=hb, mk_=mk_: s.dma_start(
                out=bass.AP(kh_sb, hb * S + 128 * mk_, [[S, 64], [512, NP], [1, 128]]),
                in_=bass.AP(kag.tensor, (64 * h + 1024 * mk_) * T,
                            [[T, 64], [128, NP], [1, 128]])),
                reads=["kag"], writes=["kh"], dma=True)
            p.emit("sync", lambda s, h=h, mk_=mk_: s.dma_start(
                out=bass.AP(vh_sb, 65 * mk_, [[64 * 65, 128], [260, NP], [1, 64]]),
                in_=bass.AP(vag.tensor, 64 * h + (2048 * mk_) * D,
                            [[D, 128], [128 * D, NP], [1, 64]])),
                reads=["vag"], writes=["vh"], dma=True)
        p.emit("sync", lambda s, h=h: s.dma_start(
            out=tsb[:], in_=bass.AP(rtab_d.tensor, h * 128 * TW, [[TW, 128], [1, TW]])),
            reads=["tsb"], writes=["tsb"], dma=True)
        p.emit("vector", lambda v: v.reduce_max(nred[:, 0:1], tsb[:], axis=AX.X),
               reads=["tsb"], writes=["nred"])
        p.emit("vector", lambda v: v.tensor_copy(
            rbc[:, 0:128], nred[:, 0:1].to_broadcast([128, 128])),
            reads=["nred"], writes=["rbc"])
        p.emit("tensor", lambda t: t.transpose(ps[6][:, 0:128], rbc[:, 0:128], mk.id_f[:]),
               reads=["rbc", "id_f"], writes=[psn[6]])
        p.emit("vector", lambda v: v.reduce_max(nred[:, 1:2], ps[6][:, 0:128], axis=AX.X),
               reads=[psn[6]], writes=["nred2"])
        p.emit("vector", lambda v: v.tensor_add(BMn[:, 0:1], Bqk[:, 0:1], nred[:, 1:2]),
               reads=["Bqk", "nred2"], writes=["BMn"])
        p.emit("vector", lambda v, h=h: v.tensor_sub(BMf[:, 0:1], BMn[:, 0:1],
                                                     ctb[:, h:h + 1]),
               reads=["BMn", "ctb"], writes=["BMf"])

        pend = [None, None]
        for ck in range(4 * NP):
            p0a = ck // 4
            pnear_hi = min(NP - 1, (2175 + 128 * ck) // 512)
            for j in range(ck // 16, 4):
                qg0 = max(4 * j, p0a)
                qcn = 4 * j + 4 - qg0
                nq = 128 * qcn
                si = sti[0] % 3
                si2 = sti[0] % 2
                sti[0] += 1
                sbk = (4, 5, 7)[si]
                st_ps, st_n = ps[sbk], psn[sbk]
                lhs = bass.AP(kh_sb, hb * S + 128 * ck, [[S, 64], [1, 128]])
                rhs = bass.AP(q_fm, hb * (DC * T) + hdc * T + 128 * qg0,
                              [[DC * T, 64], [1, nq]])
                p.emit("tensor", lambda t, lhs=lhs, rhs=rhs, st_ps=st_ps, nq=nq:
                       t.matmul(st_ps[:, 0:nq], lhs, rhs, start=True, stop=True),
                       reads=["kh", "q_fm"], writes=[st_n])
                if pend[0] is not None:
                    pend[0]()
                pend[0] = pend[1]
                stt_sb = stf[:, si * 512: si * 512 + nq]
                pt_sb = ptile[:, si * 512: si * 512 + nq]
                nearn = max(0, min(qg0 + qcn, pnear_hi + 1) - qg0)
                if nearn > 0:
                    u0 = 512 * qg0 - 128 * ck + 384
                    in1 = bass.AP(tsb, u0, [[TW, 128], [512, nearn], [1, 128]])
                    p.emit("vector", lambda v, stt_sb=stt_sb, st_ps=st_ps, in1=in1,
                           w=128 * nearn: v.scalar_tensor_tensor(
                               stt_sb[:, 0:w], st_ps[:, 0:w], BMn[:, 0:1], in1,
                               op0=ALU.subtract, op1=ALU.add),
                           reads=[st_n, "BMn", "tsb"], writes=[f"stf{si}"])
                if nearn < qcn:
                    w0, w1 = 128 * nearn, 128 * qcn
                    p.emit("vector", lambda v, stt_sb=stt_sb, st_ps=st_ps, w0=w0, w1=w1:
                           v.tensor_scalar(stt_sb[:, w0:w1], st_ps[:, w0:w1],
                                           BMf[:, 0:1], None, op0=ALU.subtract),
                           reads=[st_n, "BMf"], writes=[f"stf{si}"])
                p.emit("scalar", lambda s, pt_sb=pt_sb, stt_sb=stt_sb: s.activation(
                    pt_sb, stt_sb, AF.Exp),
                    reads=[f"stf{si}"], writes=[f"ptile{si}"])
                vt = bass.AP(vh_sb, 65 * ck, [[64 * 65, 128], [1, 65]])
                pvb, pvbn = ps[j], psn[j]
                first = (ck == 0)
                last = (ck == 16 * j + 15)
                col0 = 128 * (qg0 - 4 * j)

                def mkpv(vt=vt, pt_sb=pt_sb, pvb=pvb, pvbn=pvbn, col0=col0,
                         nq=nq, first=first, last=last, si=si):
                    p.emit("tensor", lambda t: t.matmul(
                        pvb[0:65, col0:col0 + nq], vt, pt_sb,
                        start=first, stop=last),
                        reads=[f"ptile{si}", "vh", "vh1"], writes=[pvbn])
                pend[1] = mkpv
        if pend[0] is not None:
            pend[0]()
        pend[1]()
        pend[0] = pend[1] = None
        for j in range(4):
            p.emit("vector", lambda v, j=j: v.reciprocal(
                rrow[64:65, 0:512], ps[j][64:65, 0:512]),
                reads=[psn[j]], writes=["rrow64"])
            p.emit("tensor", lambda t: t.matmul(
                ps[6][:, 0:512], mk.ones_f[64:65, :], rrow[64:65, 0:512],
                start=True, stop=True),
                reads=["rrow64", "ones_f"], writes=[psn[6]])
            p.emit("vector", lambda v: v.tensor_copy(rbc[:, 0:512], ps[6][:, 0:512]),
                   reads=[psn[6]], writes=["rbc"])
            dst = bass.AP(o_fm, hb * (DC * T) + hdc * T + 512 * j, [[DC * T, 64], [1, 512]])
            p.emit("vector", lambda v, dst=dst, j=j: v.tensor_tensor(
                dst, ps[j][0:64, 0:512], rbc[0:64, 0:512], op=ALU.mult),
                reads=[psn[j], "rbc"], writes=["o_fm"])

    load_w(wo_d)
    load_bias(bo_d)
    xt = mk.xt

    def evac_o(oc, tch, psb, psname):
        sl = oc % 2
        xs = xt[:, sl * 512: sl * 512 + 512]
        p.emit("sync", lambda s, oc=oc, tch=tch, xs=xs: s.dma_start(
            out=xs, in_=bass.AP(x_in.tensor, (oc * 128) * T + tch * 512,
                                [[T, 128], [1, 512]])),
            writes=[f"xt{sl}"], dma=True)
        osl = stf[:, (oc % 2) * 512: (oc % 2) * 512 + 512]
        osn = f"stf{oc % 2}"
        p.emit("vector", lambda v, osl=osl, psb=psb, oc=oc, xs=xs:
               v.scalar_tensor_tensor(osl, psb[:, 0:512], bia[:, oc:oc + 1], xs,
                                      op0=ALU.add, op1=ALU.add),
               reads=[psname, "bia", f"xt{sl}"], writes=[osn])
        p.emit("sync", lambda s, osl=osl, oc=oc, tch=tch: s.dma_start(
            out=bass.AP(x_out.tensor, (oc * 128) * T + tch * 512, [[T, 128], [1, 512]]),
            in_=osl),
            reads=[osn], writes=[f"xo{oc}_{tch}"], dma=True)
    mk.proj_fm(o_fm, "o_fm", w_sb, "w_sb", T, D, ps[0:4], psn[0:4], evac_o)
    p.barrier_from("gpsimd", [("syncD", p.cnt["syncD"])])


def attn_b_tables(c, rel_pos_table):
    m = c % 4
    NH = 16
    TW = 2688
    tabs = np.empty((NH, 128, TW), np.float32)
    u = np.arange(TW)[None, :]
    kk = np.arange(128)[:, None]
    delta = (u - 384) + 128 * m - kk
    idx = np.minimum(delta + 2047, 4094)
    idx = np.maximum(idx, 0)
    valid = delta >= 0
    for h in range(NH):
        t = rel_pos_table[idx, h]
        tabs[h] = np.where(valid, t, NEG)
    ctab = rel_pos_table[4094, :].reshape(1, NH).astype(np.float32)
    return tabs, ctab


def run_attn_b(xs, WQ, bQ, WK, bK, WV, bV, WO, bO, rel_pos_table):
    nc = get_program("attn_b", build_attn_b)
    wqT = np.ascontiguousarray(WQ.T)
    wkT = np.ascontiguousarray(WK.T)
    wvT = np.ascontiguousarray(WV.T)
    woT = np.ascontiguousarray(WO.T)
    bq_fm = np.ascontiguousarray(bQ.reshape(DC, 128).T)
    bk_fm = np.ascontiguousarray(bK.reshape(DC, 128).T)
    bo_fm = np.ascontiguousarray(bO.reshape(DC, 128).T)
    bvr = bV.reshape(1, D)
    in_maps = []
    for c in range(NCORE):
        tabs, ctab = attn_b_tables(c, rel_pos_table)
        in_maps.append({
            "x_in": xs[c], "wq": wqT, "wk": wkT, "wv": wvT, "wo": woT,
            "bq": bq_fm, "bk": bk_fm, "bvr": bvr, "bo": bo_fm,
            "rtab": tabs, "ctab": ctab,
        })
    res = run_bass_kernel_spmd(nc, in_maps, core_ids=list(range(NCORE)))
    return [res.results[c]["x_out"] for c in range(NCORE)]


def build_ffn1(mk, xin=None, xout=None, wsh=None, wrec=None):
    nc, p = mk.nc, mk.p
    def win(name, shape, dtype=F32):
        if wsh is not None and name in wsh:
            return wsh[name]
        apx = p.dram(name, shape, dtype, kind="ExternalInput").ap()
        if wrec is not None:
            wrec[name] = apx
        return apx

    T = TPC
    E, F = 8, 2730
    NFC = 22
    x_in = xin if xin is not None else p.dram("x_in", [D, T], F32, kind="ExternalInput").ap()
    wr_d = win("wr", [D, E], F32)
    br_d = win("br", [1, E], F32)
    wu_d = win("wu", [4, D, F], F32)
    bu_d = win("bu", [128, 4 * NFC], F32)
    wd_d = win("wd", [E, F, D], F32)
    bd_d = win("bd", [E, D], F32)
    x_out = xout if xout is not None else p.dram("x_out", [D, T], F32, kind="ExternalOutput").ap()
    smcc = p.dram("smcc", [T, E], F32).ap()
    smag = p.dram("smag", [NCORE * T, E], F32).ap()

    xln = p.sbuf("xln", [128, DC * T], BF16)
    y_sb = p.sbuf("y_sb", [128, NP * 1024], F32)
    wu_sb = p.sbuf("wu_sb", [128, 4 * DC * 128], BF16)
    wr_sb = p.sbuf("wr_sb", [128, DC * E], BF16)
    br_sb = p.sbuf("br_sb", [1, E], F32)
    bu_sb = p.sbuf("bu_sb", [128, 4 * NFC], F32)
    bd_sb = p.sbuf("bd_sb", [1, D], F32)
    sg_sb = p.sbuf("sg_sb", [128, 1024], F32)
    sm_sb = p.sbuf("sm_sb", [128, NP * E], F32)
    sm2 = p.sbuf("sm2", [128, 128 * E], F32)
    cmpt = p.sbuf("cmpt", [128, 128 * E], F32)
    cmb = p.sbuf("cmb", [128, NP * E], F32)
    lo_t = p.sbuf("lo_t", [128, E], F32)
    hi_t = p.sbuf("hi_t", [128, E], F32)
    md_t = p.sbuf("md_t", [128, E], F32)
    cnd = p.sbuf("cnd", [128, E], I32)
    pcnt = p.sbuf("pcnt", [128, E], F32)
    rx = p.sbuf("rx", [128, 1], F32)

    ps = [p.psum(f"ps{i}", [128, 512]) for i in range(8)]
    psn = [f"ps{i}" for i in range(8)]

    mk.setup_consts()
    ln = LNStream(mk, T)
    ln.run(x_in, xln, "xln", ps[0], ps[1], psn[0], psn[1])

    p.emit("gpsimd", lambda g: g.dma_start(
        out=wr_sb[:], in_=bass.AP(wr_d.tensor, 0, [[E, 128], [128 * E, DC], [1, E]])),
        writes=["wr_sb"], dma=True)
    p.emit("sync", lambda s: s.dma_start(out=br_sb[:], in_=br_d[:]),
           writes=["br_sb"], dma=True)
    for tt in range(NP):
        with p.batch("tensor"):
            for dc in range(DC):
                lhs = xln[:, dc * T + tt * 128: dc * T + tt * 128 + 128]
                rhs = wr_sb[:, dc * E: dc * E + E]
                p.emit("tensor", lambda t, lhs=lhs, rhs=rhs, dc=dc: t.matmul(
                    ps[2][:, 0:E], lhs, rhs, start=(dc == 0), stop=False),
                    reads=["xln", "wr_sb"], writes=[psn[2]])
            p.emit("tensor", lambda t: t.matmul(
                ps[2][:, 0:E], mk.ones_f[0:1, :], br_sb[0:1, 0:E], start=False, stop=True),
                reads=["br_sb", "ones_f"], writes=[psn[2]])
        sco = sm_sb[:, tt * E: tt * E + E]
        p.emit("vector", lambda v: v.reduce_max(rx[:, 0:1], ps[2][:, 0:E],
                                                axis=AX.X, negate=True),
               reads=[psn[2]], writes=["rx"])
        p.emit("scalar", lambda s, sco=sco: s.activation(
            sco, ps[2][:, 0:E], AF.Exp, bias=rx[:, 0:1], accum_out=md_t[:, 0:1]),
            reads=[psn[2], "rx"], writes=["sm_sb", "mdt0"])
        p.emit("vector", lambda v: v.reciprocal(md_t[:, 0:1], md_t[:, 0:1]),
               reads=["mdt0"], writes=["mdt0"])
        p.emit("vector", lambda v, sco=sco: v.tensor_scalar(
            sco, sco, md_t[:, 0:1], None, op0=ALU.mult),
            reads=["sm_sb", "mdt0"], writes=["sm_sb"])
    p.emit("sync", lambda s: s.dma_start(
        out=bass.AP(smcc.tensor, 0, [[E, 128], [128 * E, NP], [1, E]]), in_=sm_sb[:]),
        reads=["sm_sb"], writes=["smcc"], dma=True)
    p.barrier_from("gpsimd", [("syncD", p.cnt["syncD"])])
    p.emit("gpsimd", lambda g: g.collective_compute(
        "AllGather", ALU.bypass, replica_groups=[list(range(NCORE))],
        ins=[smcc[:]], outs=[smag[:]]),
        reads=["smcc"], writes=["smag"], coll=True)

    def emit_bisect():
        NB = NCORE * T // 128
        p.emit("vector", lambda v: v.memset(lo_t[:], 0.0), writes=["lo"])
        p.emit("vector", lambda v: v.memset(hi_t[:], 1.0), writes=["hi"])
        for cc in range(0, 128):
            pass
        p.emit("sync", lambda s: s.dma_start(
            out=sm2[:], in_=bass.AP(smag.tensor, 0, [[E, 128], [128 * E, 128], [1, E]])),
            reads=["smag"], writes=["sm2"], dma=True)
        for it in range(34):
            p.emit("vector", lambda v: v.tensor_scalar(
                md_t[:], lo_t[:], hi_t[:, 0:1] if False else 1.0, None, op0=ALU.mult),
                reads=["lo"], writes=["md"]) if False else None
            p.emit("vector", lambda v: v.tensor_add(md_t[:], lo_t[:], hi_t[:]),
                   reads=["lo", "hi"], writes=["md"])
            p.emit("vector", lambda v: v.tensor_scalar(md_t[:], md_t[:], 0.5, None,
                                                       op0=ALU.mult),
                   reads=["md"], writes=["md"])
            mdb = bass.AP(md_t, 0, [[E, 128], [0, 128], [1, E]])
            cmpo = bass.AP(cmpt, 0, [[128 * E, 128], [1, 128], [128, E]])
            p.emit("vector", lambda v, mdb=mdb, cmpo=cmpo: v.tensor_tensor(
                cmpo, sm2[:], mdb, op=ALU.is_ge),
                reads=["sm2", "md"], writes=["cmpt"])
            p.emit("vector", lambda v: v.tensor_reduce(
                pcnt[:], bass.AP(cmpt, 0, [[128 * E, 128], [128, E], [1, 128]]),
                axis=AX.X, op=ALU.add),
                reads=["cmpt"], writes=["pcnt"])
            p.emit("tensor", lambda t: t.matmul(
                ps[3][:, 0:E], mk.ones_f[:], pcnt[:], start=True, stop=True),
                reads=["pcnt", "ones_f"], writes=[psn[3]])
            p.emit("vector", lambda v: v.tensor_scalar(
                cnd[:], ps[3][:, 0:E], 2048.0, None, op0=ALU.is_ge),
                reads=[psn[3]], writes=["cnd"])
            p.emit("vector", lambda v: v.select(lo_t[:], cnd[:], md_t[:], lo_t[:]),
                   reads=["cnd", "md", "lo"], writes=["lo"])
            p.emit("vector", lambda v: v.select(hi_t[:], cnd[:], hi_t[:], md_t[:]),
                   reads=["cnd", "md", "hi"], writes=["hi"])
        lob = bass.AP(lo_t, 0, [[E, 128], [0, NP], [1, E]])
        p.emit("vector", lambda v, lob=lob: v.tensor_tensor(
            cmb[:], sm_sb[:], lob, op=ALU.is_ge),
            reads=["sm_sb", "lo"], writes=["cmb"])
        p.emit("vector", lambda v: v.tensor_tensor(cmb[:], cmb[:], sm_sb[:], op=ALU.mult),
               reads=["cmb", "sm_sb"], writes=["cmb"])

    p.emit("sync", lambda s: s.dma_start(out=bu_sb[:], in_=bu_d[:]),
           writes=["bu_sb"], dma=True)

    if DBG_FFN0:
        d1 = p.dram("dbg_comb", [128, NP * E], F32, kind="ExternalOutput").ap()
        p.emit("gpsimd", lambda g, d1=d1: g.dma_start(out=d1[:], in_=comb[:]),
               reads=["comb"], writes=["dcomb"], dma=True)
        d2 = p.dram("dbg_xln", [128, DC * T], F32, kind="ExternalOutput").ap()
        p.emit("gpsimd", lambda g, d2=d2: g.dma_start(out=d2[:], in_=xln[:]),
               reads=["xln"], writes=["dxln"], dma=True)
        d3 = p.dram("dbg_sc", [128, NP * E], F32, kind="ExternalOutput").ap()
        p.emit("gpsimd", lambda g, d3=d3: g.dma_start(out=d3[:], in_=sc_tm[:]),
               reads=["sc_tm"], writes=["dsc"], dma=True)
    for j in range(4):
        p.emit("vector", lambda v, j=j: v.memset(y_sb[:, j * 4096:(j + 1) * 4096], 0.0),
               writes=["y_sb"])

    pre = wsh is not None and "wub_b" in wsh
    if pre:
        wub_b, wdb_b = wsh["wub_b"], wsh["wdb_b"]
    else:
        wub_b = p.dram("wub_b", [4 * D, F], BF16).ap()
        wdb_b = p.dram("wdb_b", [E * F, D], BF16).ap()
        if wrec is not None:
            wrec.update(wub_b=wub_b, wdb_b=wdb_b)
    _cst = set()

    def ensure_cast_u(g):
        if pre or g in _cst:
            return
        _cst.add(g)
        p.emit("gpsimd", lambda gg, g=g: gg.dma_start(
            out=bass.AP(wub_b.tensor, g * D * F, [[F, D], [1, F]]),
            in_=bass.AP(wu_d.tensor, g * D * F, [[F, D], [1, F]])),
            writes=[f"f1cu{g}"], dma=True)

    def ensure_cast_d(e):
        if pre or ("d", e) in _cst:
            return
        _cst.add(("d", e))
        p.emit("gpsimd", lambda gg, e=e: gg.dma_start(
            out=bass.AP(wdb_b.tensor, e * F * D, [[D, F], [1, D]]),
            in_=bass.AP(wd_d.tensor, e * F * D, [[D, F], [1, D]])),
            writes=[f"f1cd{e}"], dma=True)

    fcw = lambda fc: 42 if fc == NFC - 1 else 128
    mid_d = p.dram("mid_d", [2 * NFC * 128, T], BF16).ap()
    wdr = p.sbuf("wdr", [128, 11 * D], BF16)
    midl = p.sbuf("midl", [128, 2 * 11 * 512], BF16)
    mst = p.sbuf("mst", [128, 6 * 512], BF16)
    for g in range(4):
        ensure_cast_u(g)
        ensure_cast_d(2 * g)
        ensure_cast_d(2 * g + 1)
        ensure_cast_u((g + 1) % 4)
        gs = g % 2
        for fc in range(NFC):
            w = fcw(fc)
            sl = fc % 4
            wdst = wu_sb[:, sl * DC * 128: sl * DC * 128 + DC * w]
            p.emit("sync", lambda gg, g=g, fc=fc, wdst=wdst, w=w: gg.dma_start(
                out=wdst,
                in_=bass.AP(wub_b.tensor, g * D * F + fc * 128,
                            [[F, 128], [128 * F, DC], [1, w]])),
                reads=[f"f1cu{g}"], writes=[f"wu{sl}"], dma=True)
            for tb2 in range(4):
                bank = (fc % 2) * 4 + tb2
                mps, mpsn = ps[bank], psn[bank]
                with p.batch("tensor"):
                    for dc in range(DC):
                        lhs = wu_sb[:, sl * DC * 128 + dc * w: sl * DC * 128 + dc * w + w]
                        rhs = xln[:, dc * T + tb2 * 512: dc * T + tb2 * 512 + 512]
                        p.emit("tensor", lambda t, lhs=lhs, rhs=rhs, dc=dc, mps=mps, w=w:
                               t.matmul(mps[0:w, 0:512], lhs, rhs,
                                        start=(dc == 0), stop=(dc == DC - 1)),
                               reads=[f"wu{sl}", "xln"], writes=[mpsn])
            for tb2 in range(4):
                bank = (fc % 2) * 4 + tb2
                mps, mpsn = ps[bank], psn[bank]
                msl = (fc * 4 + tb2) % 6
                msn = f"mst{msl}"
                bcol = bu_sb[:, g * NFC + fc: g * NFC + fc + 1]
                p.emit("scalar", lambda s, msl=msl, mps=mps, bcol=bcol, w=w:
                       s.activation(
                           bass.AP(mst, msl * 512, [[6 * 512, w], [1, 512]]),
                           mps[0:w, 0:512], AF.Silu, bias=bcol[0:w, 0:1]),
                       reads=[mpsn, "bu_sb"], writes=[msn])
                p.emit("scalar", lambda s, msl=msl, gs=gs, fc=fc, tb2=tb2, w=w:
                       s.dma_start(
                           out=bass.AP(mid_d.tensor,
                                       (gs * NFC * 128 + fc * 128) * T + tb2 * 512,
                                       [[T, w], [1, 512]]),
                           in_=bass.AP(mst, msl * 512, [[6 * 512, w], [1, 512]])),
                       reads=[msn], writes=[f"midd{gs}"], dma=True)
        if g == 0:
            emit_bisect()
        for e2 in range(2):
            e = 2 * g + e2
            p.emit("sync", lambda s, e=e: s.dma_start(
                out=bd_sb[:], in_=bd_d[e:e + 1, :]),
                reads=["bd_sb"], writes=["bd_sb"], dma=True)
            for icg in range(2):
                for j in range(11):
                    ic = icg * 11 + j
                    w = fcw(ic)
                    p.emit("sync", lambda s, e=e, ic=ic, j=j, w=w: s.dma_start(
                        out=bass.AP(wdr, j * D, [[11 * D, w], [1, D]]),
                        in_=bass.AP(wdb_b.tensor, e * F * D + ic * 128 * D,
                                    [[D, w], [1, D]])),
                        reads=[f"f1cd{e}", "wdr"], writes=["wdr"], dma=True)
                for tb2 in range(4):
                    mi = tb2 % 2
                    p.emit("sync", lambda s, gs=gs, icg=icg, tb2=tb2, mi=mi: s.dma_start(
                        out=bass.AP(midl, mi * 11 * 512,
                                    [[2 * 11 * 512, 128], [512, 11], [1, 512]]),
                        in_=bass.AP(mid_d.tensor,
                                    (gs * NFC * 128 + icg * 11 * 128) * T + tb2 * 512,
                                    [[T, 128], [128 * T, 11], [1, 512]])),
                        reads=[f"midd{gs}", f"midl{mi}"], writes=[f"midl{mi}"], dma=True)
                    with p.batch("tensor"):
                        for j in range(11):
                            ic = icg * 11 + j
                            w = fcw(ic)
                            for tt4 in range(4):
                                for dh in range(2):
                                    pb = ps[tt4 * 2 + dh]
                                    pbn = psn[tt4 * 2 + dh]
                                    lhs = bass.AP(midl, mi * 11 * 512 + j * 512 + tt4 * 128,
                                                  [[2 * 11 * 512, w], [1, 128]])
                                    rhs = bass.AP(wdr, j * D + dh * 512,
                                                  [[11 * D, w], [1, 512]])
                                    p.emit("tensor", lambda t, lhs=lhs, rhs=rhs, pb=pb,
                                           j=j, icg=icg: t.matmul(
                                               pb[:, 0:512], lhs, rhs,
                                               start=(j == 0),
                                               stop=(j == 10 and icg == 0)),
                                           reads=["wdr", f"midl{mi}"], writes=[pbn])
                    if icg == 1:
                        with p.batch("tensor"):
                            for tt4 in range(4):
                                for dh in range(2):
                                    pb = ps[tt4 * 2 + dh]
                                    pbn = psn[tt4 * 2 + dh]
                                    p.emit("tensor", lambda t, pb=pb, dh=dh: t.matmul(
                                        pb[:, 0:512], mk.ones_f[0:1, :],
                                        bd_sb[0:1, dh * 512: dh * 512 + 512],
                                        start=False, stop=True),
                                        reads=["bd_sb", "ones_f"], writes=[pbn])
                    for tt4 in range(4):
                        for dh in range(2):
                            pb = ps[tt4 * 2 + dh]
                            pbn = psn[tt4 * 2 + dh]
                            tt = tb2 * 4 + tt4
                            ysl = y_sb[:, tt * 1024 + dh * 512: tt * 1024 + dh * 512 + 512]
                            ccol = cmb[:, tt * E + e: tt * E + e + 1]
                            p.emit("vector", lambda v, ysl=ysl, ccol=ccol, pb=pb:
                                   v.scalar_tensor_tensor(ysl, pb[:, 0:512], ccol, ysl,
                                                          op0=ALU.mult, op1=ALU.add),
                                   reads=[pbn, "cmb", "y_sb"], writes=["y_sb"])

    xt = mk.xt
    for tch in range(4):
        for dc in range(DC):
            sl = dc % 2
            xs = xt[:, sl * 512: sl * 512 + 512]
            p.emit("sync", lambda s, dc=dc, tch=tch, xs=xs: s.dma_start(
                out=xs, in_=bass.AP(x_in.tensor, (dc * 128) * T + tch * 512,
                                    [[T, 128], [1, 512]])),
                writes=[f"xt{sl}"], dma=True)
            for j in range(4):
                tt = tch * 4 + j
                src = y_sb[:, tt * 1024 + dc * 128: tt * 1024 + dc * 128 + 128]
                tb_ = 2 + j
                p.emit("tensor", lambda t, src=src, tb_=tb_: t.transpose(
                    ps[tb_][:, 0:128], src, mk.id_f[:]),
                    reads=["y_sb", "id_f"], writes=[psn[tb_]])
                p.emit("vector", lambda v, xs=xs, j=j, tb_=tb_: v.tensor_add(
                    xs[:, j * 128: j * 128 + 128], xs[:, j * 128: j * 128 + 128],
                    ps[tb_][:, 0:128]),
                    reads=[psn[tb_], f"xt{sl}"], writes=[f"xt{sl}"])
            p.emit("sync", lambda s, dc=dc, tch=tch, xs=xs: s.dma_start(
                out=bass.AP(x_out.tensor, (dc * 128) * T + tch * 512,
                            [[T, 128], [1, 512]]), in_=xs),
                reads=[f"xt{sl}"], writes=[f"xo{dc}_{tch}"], dma=True)
    p.barrier_from("gpsimd", [("syncD", p.cnt["syncD"])])


def run_ffn1(xs, Wr, br, W_up, b_up, W_down, b_down):
    nc = get_program("ffn1", build_ffn1)
    NFC, F, E = 22, 2730, 8
    bu_fm = np.zeros((128, 4 * NFC), np.float32)
    for g in range(4):
        for fc in range(NFC):
            w = 42 if fc == NFC - 1 else 128
            bu_fm[0:w, g * NFC + fc] = b_up[g, fc * 128: fc * 128 + w]
    in_maps = []
    for c in range(NCORE):
        in_maps.append({
            "x_in": xs[c], "wr": Wr, "br": br.reshape(1, E),
            "wu": W_up, "bu": bu_fm, "wd": W_down, "bd": b_down,
        })
    res = run_bass_kernel_spmd(nc, in_maps, core_ids=list(range(NCORE)))
    return [res.results[c]["x_out"] for c in range(NCORE)]


def build_attn_b1(mk, xin=None, xout=None, wsh=None, wrec=None):
    nc, p = mk.nc, mk.p
    def win(name, shape, dtype=F32):
        if wsh is not None and name in wsh:
            return wsh[name]
        apx = p.dram(name, shape, dtype, kind="ExternalInput").ap()
        if wrec is not None:
            wrec[name] = apx
        return apx

    T = TPC
    NH, DK = 16, 64
    x_in = xin if xin is not None else p.dram("x_in", [D, T], F32, kind="ExternalInput").ap()
    wqkv_d = win("wqkv", [D, 3 * D], F32)
    bq_d = win("bq", [128, DC], F32)
    bk_d = win("bk", [128, DC], F32)
    bvr_d = win("bvr", [1, D], F32)
    wo_d = win("wo", [D, D], F32)
    bo_d = win("bo", [128, DC], F32)
    wt_d = win("wtab", [128, 384], F32)
    wt0_d = win("wtab0", [128, 384], F32)

    x_out = xout if xout is not None else p.dram("x_out", [D, T], F32, kind="ExternalOutput").ap()
    kcc = p.dram("kcc", [D, T], BF16).ap()
    vcc = p.dram("vcc", [T, D], BF16).ap()
    kag = p.dram("kag", [4 * D, T], BF16).ap()
    vag = p.dram("vag", [4 * T, D], BF16).ap()
    KSW = 2304
    kwin = p.dram("kwin", [3 * D, KSW], BF16).ap()
    vwin = p.dram("vwin", [3 * KSW, D], BF16).ap()
    kofs_d = win("kofs", [1, 3], I32)
    uofs_d = win("uofs", [1, 3], I32)
    nkl_d = p.dram("nkl", [1, 1], F32).ap()
    nkr_d = p.dram("nkr", [1, 1], F32).ap()

    xln = p.sbuf("xln", [128, DC * T], BF16)
    q_fm = p.sbuf("q_fm", [128, DC * T], BF16)
    o_fm = p.sbuf("o_fm", [128, DC * T], BF16)
    w_sb = p.sbuf("w_sb", [128, DC * D], BF16)
    kh_sb = p.sbuf("kh_sb", [128, 49 * 128], BF16)
    vh_sb = p.sbuf("vh_sb", [128, 49 * 65], BF16)
    wts = p.sbuf("wts", [128, 384], F32)
    wt0s = p.sbuf("wt0s", [128, 384], F32)
    kv_st = p.sbuf("kv_st", [128, 3 * 512], BF16)
    nsc = p.sbuf("nsc", [128, 1024], F32)
    bvbc = p.sbuf("bvbc", [128, D], F32)
    ptile = p.sbuf("ptile", [128, 2 * 384], BF16)
    ptt = p.sbuf("ptt", [128, 6 * 128], BF16)
    stf = p.sbuf("stf", [128, 4 * 512], F32)
    rrow = p.sbuf("rrow", [128, 512], F32)
    rbc = p.sbuf("rbc", [128, 512], F32)
    nq2 = p.sbuf("nq2", [128, 1], F32)
    nk2 = p.sbuf("nk2", [128, 1], F32)
    nred = p.sbuf("nred", [128, 4], F32)
    Bqk = p.sbuf("Bqk", [128, 1], F32)
    bia = p.sbuf("bia", [128, DC], F32)
    mofs = p.sbuf("mofs", [1, 6], I32)

    ps = [p.psum(f"ps{i}", [128, 512]) for i in range(8)]
    psn = [f"ps{i}" for i in range(8)]

    mk.setup_consts()
    ln = LNStream(mk, T)
    ln.run(x_in, xln, "xln", ps[0], ps[1], psn[0], psn[1])

    p.emit("gpsimd", lambda g: g.dma_start(out=mofs[0:1, 0:3], in_=kofs_d[:]),
           writes=["mofs"], dma=True)
    p.emit("gpsimd", lambda g: g.dma_start(out=mofs[0:1, 3:6], in_=uofs_d[:]),
           writes=["mofs"], dma=True)

    def load_w(col0):
        p.emit("gpsimd", lambda g, col0=col0: g.dma_start(
            out=w_sb[:], in_=bass.AP(wqkv_d.tensor, col0,
                                     [[3 * D, 128], [128 * 3 * D, DC], [1, D]])),
            writes=["w_sb"], dma=True)

    def load_bias(dram_ap):
        p.emit("sync", lambda s, d=dram_ap: s.dma_start(out=bia[:], in_=d[:]),
               writes=["bia"], dma=True)

    load_w(0)
    load_bias(bq_d)

    def evac_q(oc, tch, psb, psname):
        dst = q_fm[:, oc * T + tch * 512: oc * T + tch * 512 + 512]
        p.emit("vector", lambda v, dst=dst, psb=psb, oc=oc: v.tensor_scalar(
            dst, psb[:, 0:512], bia[:, oc:oc + 1], None, op0=ALU.add),
            reads=[psname, "bia"], writes=["q_fm"])
    mk.proj_fm(xln, "xln", w_sb, "w_sb", T, D, ps[0:4], psn[0:4], evac_q)
    for tch in range(4):
        for dc in range(DC):
            sl = dc % 2
            sq = nsc[:, sl * 512: sl * 512 + 512]
            src = q_fm[:, dc * T + tch * 512: dc * T + tch * 512 + 512]
            p.emit("scalar", lambda s, sq=sq, src=src: s.activation(sq, src, AF.Square),
                   reads=["q_fm"], writes=[f"nsc{sl}"])
            p.emit("tensor", lambda t, sq=sq, dc=dc: t.matmul(
                ps[4][:, 0:512], mk.ones_f[:], sq, start=(dc == 0), stop=(dc == DC - 1)),
                reads=[f"nsc{sl}", "ones_f"], writes=[psn[4]])
        p.emit("vector", lambda v, tch=tch: v.reduce_max(
            nred[:, tch:tch + 1], ps[4][:, 0:512], axis=AX.X),
            reads=[psn[4]], writes=["nred"])
    p.emit("vector", lambda v: v.reduce_max(nq2[:, 0:1], nred[:, 0:4], axis=AX.X),
           reads=["nred"], writes=["nq2"])

    load_w(D)
    load_bias(bk_d)

    def evac_k(oc, tch, psb, psname):
        sl = (oc + tch) % 3
        st = kv_st[:, sl * 512: sl * 512 + 512]
        stn = f"kv{sl}"
        p.emit("vector", lambda v, st=st, psb=psb, oc=oc: v.tensor_scalar(
            st, psb[:, 0:512], bia[:, oc:oc + 1], 0.125, op0=ALU.add, op1=ALU.mult),
            reads=[psname, "bia"], writes=[stn])
        p.emit("sync", lambda s, st=st, oc=oc, tch=tch: s.dma_start(
            out=bass.AP(kcc.tensor, (oc * 128) * T + tch * 512, [[T, 128], [1, 512]]),
            in_=st),
            reads=[stn], writes=[f"kcc{oc}_{tch}"], dma=True)
        ssl = oc % 2
        sq = nsc[:, ssl * 512: ssl * 512 + 512]
        p.emit("scalar", lambda s, sq=sq, st=st: s.activation(sq, st, AF.Square),
               reads=[stn], writes=[f"nsc{ssl}"])
        p.emit("tensor", lambda t, sq=sq, oc=oc: t.matmul(
            ps[5][:, 0:512], mk.ones_f[:], sq, start=(oc == 0), stop=(oc == DC - 1)),
            reads=[f"nsc{ssl}", "ones_f"], writes=[psn[5]])
        if oc == DC - 1:
            p.emit("vector", lambda v, tch=tch: v.reduce_max(
                nred[:, tch:tch + 1], ps[5][:, 0:512], axis=AX.X),
                reads=[psn[5]], writes=["nred"])
    mk.proj_fm(xln, "xln", w_sb, "w_sb", T, D, ps[0:4], psn[0:4], evac_k)
    p.emit("vector", lambda v: v.reduce_max(nk2[:, 0:1], nred[:, 0:4], axis=AX.X),
           reads=["nred"], writes=["nk2"])

    load_w(2 * D)
    p.emit("sync", lambda s: s.dma_start(out=rrow[0:1, 0:512], in_=bvr_d[0:1, 0:512]),
           writes=["bvr0"], dma=True)
    p.emit("sync", lambda s: s.dma_start(out=nsc[0:1, 0:512], in_=bvr_d[0:1, 512:1024]),
           writes=["bvr1"], dma=True)
    for half in range(2):
        src = rrow[0:1, 0:512] if half == 0 else nsc[0:1, 0:512]
        rname = f"bvr{half}"
        p.emit("tensor", lambda t, src=src: t.matmul(
            ps[6][:, 0:512], mk.ones_f[0:1, :], src, start=True, stop=True),
            reads=[rname, "ones_f"], writes=[psn[6]])
        p.emit("vector", lambda v, half=half: v.tensor_copy(
            bvbc[:, half * 512: half * 512 + 512], ps[6][:, 0:512]),
            reads=[psn[6]], writes=["bvbc"])

    def evac_v(tt, oc, psb, psname):
        sl = (tt * 2 + oc) % 3
        st = kv_st[:, sl * 512: sl * 512 + 512]
        stn = f"kv{sl}"
        p.emit("vector", lambda v, st=st, psb=psb, oc=oc: v.tensor_tensor(
            st, psb[:, 0:512], bvbc[:, oc * 512: oc * 512 + 512], op=ALU.add),
            reads=[psname, "bvbc"], writes=[stn])
        p.emit("sync", lambda s, st=st, tt=tt, oc=oc: s.dma_start(
            out=bass.AP(vcc.tensor, (tt * 128) * D + oc * 512, [[D, 128], [1, 512]]),
            in_=st),
            reads=[stn], writes=[f"vcc{tt}_{oc}"], dma=True)
    mk.proj_tm(xln, "xln", w_sb, "w_sb", T, D, ps[0:4], psn[0:4], evac_v)

    p.barrier_from("gpsimd", [("syncD", p.cnt["syncD"])])
    p.emit("gpsimd", lambda g: g.dma_start(out=nkl_d[:], in_=nk2[0:1, 0:1]),
           reads=["nk2"], writes=["nkl"], dma=True)
    p.emit("gpsimd", lambda g: g.collective_compute(
        "AllGather", ALU.bypass, replica_groups=GROUPS, ins=[kcc[:]], outs=[kag[:]]),
        reads=["kcc"], writes=["kag"], coll=True)
    p.emit("gpsimd", lambda g: g.collective_compute(
        "AllGather", ALU.bypass, replica_groups=GROUPS, ins=[vcc[:]], outs=[vag[:]]),
        reads=["vcc"], writes=["vag"], coll=True)
    p.emit("gpsimd", lambda g: g.collective_compute(
        "AllReduce", ALU.max, replica_groups=GROUPS, ins=[nkl_d[:]], outs=[nkr_d[:]]),
        reads=["nkl"], writes=["nkr"], coll=True)
    p.emit("gpsimd", lambda g: g.dma_start(out=nred[0:1, 0:1], in_=nkr_d[:]),
           reads=["nkr", "nred"], writes=["nred"], dma=True)
    p.emit("tensor", lambda t: t.matmul(
        ps[6][:, 0:1], mk.ones_f[0:1, :], nred[0:1, 0:1], start=True, stop=True),
        reads=["nred", "ones_f"], writes=[psn[6]])
    p.emit("vector", lambda v: v.tensor_tensor(Bqk[:, 0:1], nq2[:, 0:1],
                                               ps[6][:, 0:1], op=ALU.mult),
           reads=["nq2", psn[6]], writes=["Bqk"])
    p.emit("scalar", lambda s: s.activation(Bqk[:, 0:1], Bqk[:, 0:1], AF.Sqrt),
           reads=["Bqk"], writes=["Bqk"])

    p.emit("sync", lambda s: s.dma_start(out=wts[:], in_=wt_d[:]),
           writes=["wts"], dma=True)
    p.emit("sync", lambda s: s.dma_start(out=wt0s[:], in_=wt0_d[:]),
           writes=["wt0s"], dma=True)

    p.emit("vector", lambda v: v.memset(
        bass.AP(vh_sb, 64, [[49 * 65, 128], [65, 49], [1, 1]]), 1.0),
        writes=["vh1"])

    moff = [None]
    zt = p.sbuf("zt", [128, 512], BF16)
    p.emit("vector", lambda v: v.memset(zt[:], 0.0), writes=["zt"])
    for j in range(3):
        p.emit("sync", lambda s, j=j: s.dma_start(
            out=bass.AP(kwin.tensor, (1024 * j) * KSW + 128,
                        [[KSW, 128], [128 * KSW, 8], [1, 128]]),
            in_=bass.AP(zt, 0, [[512, 128], [0, 8], [1, 128]])),
            reads=["zt"], writes=["kwin"], dma=True)
        p.emit("sync", lambda s, j=j: s.dma_start(
            out=bass.AP(vwin.tensor, (KSW * j + 128) * D, [[D, 128], [1, D]]),
            in_=bass.AP(zt, 0, [[512, 128], [0, 2], [1, 512]])),
            reads=["zt"], writes=["vwin"], dma=True)

    def ldofs(g):
        if moff[0] is None:
            moff[0] = [
                mk.nc.values_load(mofs[0:1, i:i + 1], engines=[ET.Pool],
                                  min_val=0, max_val=7 * 1024 * 2048,
                                  skip_runtime_bounds_check=True)
                for i in range(6)]
        return moff[0]

    for j in range(3):
        def kcopy(g, j=j):
            mo = ldofs(g)
            return g.dma_start(
                out=bass.AP(kwin.tensor, (1024 * j) * KSW + mo[3 + j],
                            [[KSW, 1024], [1, T]]),
                in_=bass.AP(kag.tensor, mo[j], [[T, 1024], [1, T]]))
        p.emit("gpsimd", kcopy, reads=["kag", "mofs", "kwin"], writes=["kwin"], dma=True)

        def vcopy(g, j=j):
            mo = ldofs(g)
            return g.dma_start(
                out=bass.AP(vwin.tensor, (KSW * j) * D + mo[3 + j] * D,
                            [[D, T], [1, D]]),
                in_=bass.AP(vag.tensor, mo[j], [[D, T], [1, D]]))
        p.emit("gpsimd", vcopy, reads=["vag", "mofs", "vwin"], writes=["vwin"], dma=True)

    sti = [0]
    for h in range(NH):
        hb = 64 * (h % 2)
        hdc = h // 2
        for j in range(3):
            p.emit("sync", lambda s, h=h, hb=hb, j=j: s.dma_start(
                out=bass.AP(kh_sb, hb * (49 * 128) + 128 * j,
                            [[49 * 128, 64], [384, NP], [1, 128]]),
                in_=bass.AP(kwin.tensor, (1024 * j + 64 * h) * KSW + 128,
                            [[KSW, 64], [128, NP], [1, 128]])),
                reads=["kwin"], writes=["kh"], dma=True)
            p.emit("sync", lambda s, h=h, j=j: s.dma_start(
                out=bass.AP(vh_sb, 65 * j, [[49 * 65, 128], [195, NP], [1, 64]]),
                in_=bass.AP(vwin.tensor, (KSW * j + 128) * D + 64 * h,
                            [[D, 128], [128 * D, NP], [1, 64]])),
                reads=["vwin"], writes=["vh"], dma=True)
        pend = [None]
        for pq in range(NP):
            bank = pq // 4
            col0 = 128 * (pq % 4)
            pvb, pvbn = ps[bank], psn[bank]
            si = sti[0] % 2
            sti[0] += 1
            st_ps, st_n = ps[4 + si], psn[4 + si]
            lhs = bass.AP(q_fm, hb * (DC * T) + hdc * T + 128 * pq,
                          [[DC * T, 64], [1, 128]])
            rhs = bass.AP(kh_sb, hb * (49 * 128) + 384 * pq, [[49 * 128, 64], [1, 384]])
            p.emit("tensor", lambda t, lhs=lhs, rhs=rhs, st_ps=st_ps: t.matmul(
                st_ps[:, 0:384], lhs, rhs, start=True, stop=True),
                reads=["kh", "q_fm"], writes=[st_n])
            if pend[0] is not None:
                pend[0]()
            stt_sb = stf[:, si * 512: si * 512 + 384]
            wsrc = wt0s if pq == 0 else wts
            rdn = "wt0s" if pq == 0 else "wts"
            p.emit("vector", lambda v, stt_sb=stt_sb, st_ps=st_ps, wsrc=wsrc:
                   v.tensor_tensor(stt_sb, st_ps[:, 0:384], wsrc[:, 0:384], op=ALU.add),
                   reads=[st_n, rdn], writes=[f"stf{si}"])
            p.emit("vector", lambda v, stt_sb=stt_sb, si=si: v.reduce_max(
                rrow[:, 4 * si:4 * si + 1], stt_sb, axis=AX.X, negate=True),
                reads=[f"stf{si}"], writes=[f"rmx{si}"])
            p.emit("scalar", lambda s, stt_sb=stt_sb, si=si: s.activation(
                stt_sb, stt_sb, AF.Exp, bias=rrow[:, 4 * si:4 * si + 1],
                accum_out=rrow[:, 4 * si + 1:4 * si + 2]),
                reads=[f"stf{si}", f"rmx{si}"], writes=[f"stf{si}", f"rsm{si}"])
            p.emit("vector", lambda v, si=si: v.reciprocal(
                rrow[:, 4 * si + 2:4 * si + 3], rrow[:, 4 * si + 1:4 * si + 2]),
                reads=[f"rsm{si}"], writes=[f"rrec{si}"])
            pn_sb = stf[:, (2 + si) * 512: (2 + si) * 512 + 384]
            p.emit("vector", lambda v, stt_sb=stt_sb, pn_sb=pn_sb, si=si: v.tensor_scalar(
                pn_sb, stt_sb, rrow[:, 4 * si + 2:4 * si + 3], None, op0=ALU.mult),
                reads=[f"stf{si}", f"rrec{si}"], writes=[f"stfo{si}"])

            def mkpv(si=si, pq=pq, pvb=pvb, pvbn=pvbn, col0=col0):
                for j in range(3):
                    tb = 6 + (j % 2)
                    pnj = stf[:, (2 + si) * 512 + 128 * j: (2 + si) * 512 + 128 * j + 128]
                    p.emit("tensor", lambda t, pnj=pnj, tb=tb: t.transpose(
                        ps[tb][:, 0:128], pnj, mk.id_f[:]),
                        reads=[f"stfo{si}", "id_f"], writes=[psn[tb]])
                    pts = ptt[:, (si * 3 + j) * 128: (si * 3 + j) * 128 + 128]
                    ptn = f"ptt{si}_{j}"
                    p.emit("vector", lambda v, pts=pts, tb=tb: v.tensor_copy(
                        pts, ps[tb][:, 0:128]),
                        reads=[psn[tb]], writes=[ptn])
                    vt = bass.AP(vh_sb, 65 * (3 * pq + j), [[49 * 65, 128], [1, 64]])
                    p.emit("tensor", lambda t, vt=vt, pvb=pvb, col0=col0, j=j, pts=pts:
                           t.matmul(pvb[0:64, col0:col0 + 128], vt, pts,
                                    start=(j == 0), stop=(j == 2)),
                           reads=[ptn, "vh"], writes=[pvbn])
            pend[0] = mkpv
        pend[0]()
        pend[0] = None
        for j in range(4):
            dst = bass.AP(o_fm, hb * (DC * T) + hdc * T + 512 * j, [[DC * T, 64], [1, 512]])
            p.emit("vector", lambda v, dst=dst, j=j: v.tensor_copy(
                dst, ps[j][0:64, 0:512]),
                reads=[psn[j]], writes=["o_fm"])

    p.emit("gpsimd", lambda g: g.dma_start(
        out=w_sb[:], in_=bass.AP(wo_d.tensor, 0, [[D, 128], [128 * D, DC], [1, D]])),
        writes=["w_sb"], dma=True)
    load_bias(bo_d)
    xt = mk.xt

    def evac_o(oc, tch, psb, psname):
        sl = oc % 2
        xs = xt[:, sl * 512: sl * 512 + 512]
        p.emit("sync", lambda s, oc=oc, tch=tch, xs=xs: s.dma_start(
            out=xs, in_=bass.AP(x_in.tensor, (oc * 128) * T + tch * 512,
                                [[T, 128], [1, 512]])),
            writes=[f"xt{sl}"], dma=True)
        osl = stf[:, (2 + oc % 2) * 512: (2 + oc % 2) * 512 + 512]
        osn = f"stfo{oc % 2}"
        p.emit("vector", lambda v, osl=osl, psb=psb, oc=oc, xs=xs:
               v.scalar_tensor_tensor(osl, psb[:, 0:512], bia[:, oc:oc + 1], xs,
                                      op0=ALU.add, op1=ALU.add),
               reads=[psname, "bia", f"xt{sl}"], writes=[osn])
        p.emit("sync", lambda s, osl=osl, oc=oc, tch=tch: s.dma_start(
            out=bass.AP(x_out.tensor, (oc * 128) * T + tch * 512, [[T, 128], [1, 512]]),
            in_=osl),
            reads=[osn], writes=[f"xo{oc}_{tch}"], dma=True)
    mk.proj_fm(o_fm, "o_fm", w_sb, "w_sb", T, D, ps[0:4], psn[0:4], evac_o)
    if DBG_B1:
        dbg1 = p.dram("dbg1", [D, 2304], F32, kind="ExternalOutput").ap()
        dbg2 = p.dram("dbg2", [128, 49 * 128], F32, kind="ExternalOutput").ap()
        dbg3 = p.dram("dbg3", [128, DC * T], F32, kind="ExternalOutput").ap()
        p.emit("gpsimd", lambda g: g.dma_start(
            out=dbg1[:], in_=bass.AP(kwin.tensor, 0, [[2304, 1024], [1, 2304]])),
            reads=["kwin"], writes=["dbg1"], dma=True)
        p.emit("gpsimd", lambda g: g.dma_start(out=dbg2[:], in_=kh_sb[:]),
               reads=["kh"], writes=["dbg2"], dma=True)
        p.emit("gpsimd", lambda g: g.dma_start(out=dbg3[:], in_=o_fm[:]),
               reads=["o_fm"], writes=["dbg3"], dma=True)
        dbg4 = p.dram("dbg4", [3, 128, 1], F32, kind="ExternalOutput").ap()
        p.emit("gpsimd", lambda g: g.dma_start(out=dbg4[0], in_=Bqk[:, 0:1]),
               reads=["Bqk"], writes=["dbg4"], dma=True)
        p.emit("gpsimd", lambda g: g.dma_start(out=dbg4[1], in_=nq2[:, 0:1]),
               reads=["nq2"], writes=["dbg4"], dma=True)
        p.emit("gpsimd", lambda g: g.dma_start(out=dbg4[2], in_=nk2[:, 0:1]),
               reads=["nk2"], writes=["dbg4"], dma=True)
        dbg5 = p.dram("dbg5", [128, 2048], F32, kind="ExternalOutput").ap()
        p.emit("gpsimd", lambda g: g.dma_start(out=dbg5[:], in_=stf[:]),
               reads=["stf0", "stf1"], writes=["dbg5"], dma=True)
        dbg6 = p.dram("dbg6", [128, 512], F32, kind="ExternalOutput").ap()
        p.emit("gpsimd", lambda g: g.dma_start(out=dbg6[:], in_=rbc[:, 0:512]),
               reads=["rbc"], writes=["dbg6"], dma=True)
        dbg7 = p.dram("dbg7", [128, 256], F32, kind="ExternalOutput").ap()
        p.emit("gpsimd", lambda g: g.dma_start(out=dbg7[:], in_=ptile[:]),
               reads=["ptile0", "ptile1"], writes=["dbg7"], dma=True)
        dbg8 = p.dram("dbg8", [128, 512], F32, kind="ExternalOutput").ap()
        p.emit("vector", lambda v: v.tensor_copy(stf[:, 1536:2048], ps[3][:, 0:512]),
               reads=["ps3"], writes=["dbg8s"])
        p.emit("gpsimd", lambda g: g.dma_start(out=dbg8[:], in_=stf[:, 1536:2048]),
               reads=["dbg8s"], writes=["dbg8"], dma=True)
        dbg9 = p.dram("dbg9", [3 * KSW, D], F32, kind="ExternalOutput").ap()
        p.emit("gpsimd", lambda g: g.dma_start(out=dbg9[:], in_=vwin[:]),
               reads=["vwin"], writes=["dbg9"], dma=True)
        dbg10 = p.dram("dbg10", [128, 49 * 65], F32, kind="ExternalOutput").ap()
        p.emit("gpsimd", lambda g: g.dma_start(out=dbg10[:], in_=vh_sb[:]),
               reads=["vh", "vh1"], writes=["dbg10"], dma=True)
        p.barrier_from("gpsimd", [("gpsimdD", p.cnt["gpsimdD"])])
    p.barrier_from("gpsimd", [("syncD", p.cnt["syncD"])])


DBG_B1 = False
DBG_DELTA = False
DBG_FFN0 = False


def attn_b1_tables(c):
    m = c % 4
    qq = np.arange(128)[:, None]
    kk = np.arange(128)[None, :]
    wt = np.empty((128, 384), np.float32)
    wt0 = np.empty((128, 384), np.float32)
    for j in range(3):
        delta = 128 * j + qq - kk
        tile = np.where((delta >= 0) & (delta < 256), 0.0, NEG)
        wt[:, 128 * j:128 * j + 128] = tile
        wt0[:, 128 * j:128 * j + 128] = tile if j <= m else NEG
    return wt, wt0


def run_attn_b1(xs, W_qkv, b_qkv, W_out, b_out):
    nc = get_program("attn_b1", build_attn_b1)
    woT = np.ascontiguousarray(W_out)
    bq_fm = np.ascontiguousarray(b_qkv[0:D].reshape(DC, 128).T)
    bk_fm = np.ascontiguousarray(b_qkv[D:2 * D].reshape(DC, 128).T)
    bvr = b_qkv[2 * D:3 * D].reshape(1, D)
    bo_fm = np.ascontiguousarray(b_out.reshape(DC, 128).T)
    in_maps = []
    for c in range(NCORE):
        wt, wt0 = attn_b1_tables(c)
        m = c % 4
        kofs = np.array([[((m - j) % 4) * 2097152 for j in range(3)]], np.int32)
        uofs = np.array([[128 if m >= j else 256 for j in range(3)]], np.int32)
        in_maps.append({
            "x_in": xs[c], "wqkv": W_qkv, "bq": bq_fm, "bk": bk_fm, "bvr": bvr,
            "wo": woT, "bo": bo_fm, "wtab": wt, "wtab0": wt0,
            "kofs": kofs, "uofs": uofs,
        })
    res = run_bass_kernel_spmd(nc, in_maps, core_ids=list(range(NCORE)))
    return [res.results[c]["x_out"] for c in range(NCORE)]


def build_delta_scan(mk, xin_ag=None, og_out=None, og_perm=False,
                     wsh=None, wrec=None):
    nc, p = mk.nc, mk.p
    def win(name, shape, dtype=F32):
        if wsh is not None and name in wsh:
            return wsh[name]
        apx = p.dram(name, shape, dtype, kind="ExternalInput").ap()
        if wrec is not None:
            wrec[name] = apx
        return apx

    TB = 2048
    SEQ = S
    x_in = None if xin_ag is not None else p.dram("x_in", [D, SEQ], F32, kind="ExternalInput").ap()
    wq_d = win("wq", [D, 512], F32)
    wk_d = win("wk", [D, 512], F32)
    wv_d = win("wv", [D, 256], F32)
    wg_d = win("wg", [D, 256], F32)
    wa_d = win("wa", [D, 4], F32)
    wb_d = win("wb", [D, 4], F32)
    ba_d = win("ba", [1, 4], F32)
    bb_d = win("bb", [1, 4], F32)
    og_d = og_out if og_out is not None else p.dram("og", [256, SEQ], F32, kind="ExternalOutput").ap()

    xln = p.sbuf("xln", [128, DC * TB], BF16)
    wq_sb = p.sbuf("wq_sb", [128, DC * 512], BF16)
    wk_sb = p.sbuf("wk_sb", [128, DC * 512], BF16)
    wv_sb = p.sbuf("wv_sb", [128, DC * 256], BF16)
    wg_sb = p.sbuf("wg_sb", [128, DC * 256], BF16)
    wa_sb = p.sbuf("wa_sb", [128, DC * 4], BF16)
    wb_sb = p.sbuf("wb_sb", [128, DC * 4], BF16)
    bab_sb = p.sbuf("bab_sb", [1, 8], F32)
    q_fm = p.sbuf("q_fm", [128, 4 * TB], BF16)
    k_fm = p.sbuf("k_fm", [128, 4 * TB], BF16)
    v_tm = p.sbuf("v_tm", [128, 16 * 256], BF16)
    g_tm = p.sbuf("g_tm", [128, 16 * 256], BF16)
    za_sb = p.sbuf("za_sb", [4, TB], F32)
    zb_sb = p.sbuf("zb_sb", [4, TB], F32)
    lg_sb = p.sbuf("lg_sb", [4, TB], F32)
    bt_sb = p.sbuf("bt_sb", [4, TB], F32)
    lsc = p.sbuf("lsc", [4, TB], F32)
    cb8 = p.sbuf("cb8", [128, 8], F32)
    gam4 = p.sbuf("gam4", [128, 4], F32)
    lrt4 = p.sbuf("lrt4", [128, 4 * 128], F32)
    t14 = p.sbuf("t14", [128, 4 * 128], F32)
    pp4 = p.sbuf("pp4", [128, 4 * 128], F32)
    p24 = p.sbuf("p24", [128, 4 * 128], F32)
    a4 = p.sbuf("a4", [128, 4 * 128], BF16)
    at4 = p.sbuf("at4", [128, 28 * 128], BF16)
    a24 = p.sbuf("a24", [128, 4 * 128], BF16)
    wt4 = p.sbuf("wt4", [128, 4 * 128], BF16)
    x4 = p.sbuf("x4", [128, 4 * 64], BF16)
    ktm4 = p.sbuf("ktm4", [128, 4 * 64], BF16)
    kz4 = p.sbuf("kz4", [128, 4 * 64], BF16)
    o14 = p.sbuf("o14", [128, 4 * 64], F32)
    og4 = p.sbuf("og4", [128, 4 * 64], F32)
    ogst = p.sbuf("ogst", [64, 4 * 128], F32)
    zst = p.sbuf("zst", [64, 4 * 64], F32)
    zb16 = p.sbuf("zb16", [64, 4 * 64], BF16)
    rc4 = p.sbuf("rc4", [128, 8], F32)
    mstrict = p.sbuf("mstrict", [128, 128], F32)
    mupper = p.sbuf("mupper", [128, 128], F32)
    tscr = p.sbuf("tscr", [128, 512], F32)

    ps = [p.psum(f"ps{i}", [128, 512]) for i in range(6)]
    ps6b = p.psum("ps6b", [128, 1024], BF16)
    ps7b = p.psum("ps7b", [128, 1024], BF16)
    psn = [f"ps{i}" for i in range(6)] + ["ps6b", "ps7b"]
    bfb = [(ps6b, "ps6b"), (ps7b, "ps7b")]

    mk.setup_consts()
    p.emit("gpsimd", lambda g: g.memset(mstrict[:], 0.0), writes=["mstrict"])
    p.emit("gpsimd", lambda g: g.affine_select(
        out=mstrict[:], in_=mstrict[:], compare_op=ALU.is_gt, fill=NEG,
        base=0, pattern=[[-1, 128]], channel_multiplier=1),
        reads=["mstrict"], writes=["mstrict"])
    p.emit("gpsimd", lambda g: g.memset(mupper[:], 0.0), writes=["mupper"])
    p.emit("gpsimd", lambda g: g.affine_select(
        out=mupper[:], in_=mupper[:], compare_op=ALU.is_ge, fill=NEG,
        base=0, pattern=[[1, 128]], channel_multiplier=-1),
        reads=["mupper"], writes=["mupper"])
    p.emit("vector", lambda v: v.memset(zst[:], 0.0), writes=["zst"])
    p.emit("vector", lambda v: v.memset(zb16[:], 0.0), writes=["zb16"])

    def loadw(dst, dname, dram_ap, OUT):
        p.emit("gpsimd", lambda g, d=dram_ap, dst=dst, OUT=OUT: g.dma_start(
            out=dst[:], in_=bass.AP(d.tensor, 0, [[OUT, 128], [128 * OUT, DC], [1, OUT]])),
            writes=[dname], dma=True)
    loadw(wq_sb, "wq_sb", wq_d, 512)
    loadw(wk_sb, "wk_sb", wk_d, 512)
    loadw(wv_sb, "wv_sb", wv_d, 256)
    loadw(wg_sb, "wg_sb", wg_d, 256)
    loadw(wa_sb, "wa_sb", wa_d, 4)
    loadw(wb_sb, "wb_sb", wb_d, 4)
    p.emit("sync", lambda s: s.dma_start(out=bab_sb[0:1, 0:4], in_=ba_d[:]),
           writes=["bab"], dma=True)
    p.emit("sync", lambda s: s.dma_start(out=bab_sb[0:1, 4:8], in_=bb_d[:]),
           writes=["bab"], dma=True)

    ln = LNStream(mk, TB)
    EPS2 = 1e-12

    for tb in range(4):
        if xin_ag is None:
            ln.run(x_in, xln, "xln", ps[0], ps[1], psn[0], psn[1],
                   rstride=SEQ, col0=tb * TB)
        else:
            def srcfn(dc, tch, tb=tb):
                pidx = tb * 4 + tch
                return bass.AP(xin_ag.tensor, (dc * 128) * TPC + 128 * pidx,
                               [[TPC, 128], [1024 * TPC, 4], [1, 128]])
            ln.run(None, xln, "xln", ps[0], ps[1], psn[0], psn[1], srcfn=srcfn)

        def evac_q(oc, tch, psb, psname, dst, dname):
            d = dst[:, oc * TB + tch * 512: oc * TB + tch * 512 + 512]
            p.emit("vector", lambda v, d=d, psb=psb: v.tensor_copy(d, psb[:, 0:512]),
                   reads=[psname], writes=[dname])
        mk.proj_fm(xln, "xln", wq_sb, "wq_sb", TB, 512, ps[0:4], psn[0:4],
                   lambda oc, tch, psb, psname: evac_q(oc, tch, psb, psname, q_fm, "q_fm"))
        mk.proj_fm(xln, "xln", wk_sb, "wk_sb", TB, 512, ps[0:4], psn[0:4],
                   lambda oc, tch, psb, psname: evac_q(oc, tch, psb, psname, k_fm, "k_fm"))

        it2 = 0
        for (src_t, sname) in ((q_fm, "q_fm"), (k_fm, "k_fm")):
            for oc in range(4):
                for tch in range(4):
                    pb_, pbn_ = ps[2 + it2 % 4], psn[2 + it2 % 4]
                    ssl = it2 % 2
                    tsl = 0
                    it2 += 1
                    seg = bass.AP(src_t, oc * TB + tch * 512, [[4 * TB, 64], [1, 512]])
                    sqs = mk.ln_scr[0:64, ssl * 512: ssl * 512 + 512]
                    tss = tscr[:, tsl * 512: tsl * 512 + 512]
                    tsn = f"tscr{tsl}"
                    p.emit("scalar", lambda s, seg=seg, sqs=sqs: s.activation(
                        sqs, seg, AF.Square),
                        reads=[sname], writes=[f"ln_scr{ssl}"])
                    p.emit("tensor", lambda t, sqs=sqs, pb_=pb_: t.matmul(
                        pb_[:, 0:512], mk.ones_f[0:64, :],
                        sqs, start=True, stop=True),
                        reads=[f"ln_scr{ssl}", "ones_f"], writes=[pbn_])
                    p.emit("vector", lambda v, tss=tss, pb_=pb_: v.tensor_scalar(
                        tss, pb_[:, 0:512], EPS2, None, op0=ALU.max),
                        reads=[pbn_], writes=[tsn])
                    p.emit("scalar", lambda s, tss=tss: s.activation(
                        tss, tss, AF.Sqrt),
                        reads=[tsn], writes=[tsn])
                    p.emit("vector", lambda v, tss=tss: v.reciprocal(
                        tss, tss),
                        reads=[tsn], writes=[tsn])
                    p.emit("vector", lambda v, seg=seg, tss=tss: v.tensor_tensor(
                        seg, seg, tss[0:64, 0:512], op=ALU.mult),
                        reads=[tsn, sname], writes=[sname])

        def evac_tm(tt, oc, psb, psname, dst, dname, act=None):
            d = dst[:, tt * 256 + oc * 512: tt * 256 + oc * 512 + 512]
            if act is None:
                p.emit("vector", lambda v, d=d, psb=psb: v.tensor_copy(d, psb[:, 0:512]),
                       reads=[psname], writes=[dname])
            else:
                p.emit("scalar", lambda s, d=d, psb=psb, act=act: s.activation(
                    d, psb[:, 0:512], act),
                    reads=[psname], writes=[dname])
        for (wsb, wname, dst, dname, act) in (
                (wv_sb, "wv_sb", v_tm, "v_tm", None),
                (wg_sb, "wg_sb", g_tm, "g_tm", AF.Silu)):
            for tt in range(NP):
                pb_, pbn_ = ps[2 + tt % 4], psn[2 + tt % 4]
                with p.batch("tensor"):
                    for dc in range(DC):
                        lhs = xln[:, dc * TB + tt * 128: dc * TB + tt * 128 + 128]
                        rhs = wsb[:, dc * 256: dc * 256 + 256]
                        p.emit("tensor", lambda t, lhs=lhs, rhs=rhs, dc=dc, pb_=pb_:
                               t.matmul(pb_[:, 0:256], lhs, rhs,
                                        start=(dc == 0), stop=(dc == DC - 1)),
                               reads=[wname, "xln"], writes=[pbn_])
                d = dst[:, tt * 256: tt * 256 + 256]
                if act is None:
                    p.emit("vector", lambda v, d=d, pb_=pb_: v.tensor_copy(
                        d, pb_[:, 0:256]),
                        reads=[pbn_], writes=[dname])
                else:
                    p.emit("scalar", lambda s, d=d, act=act, pb_=pb_: s.activation(
                        d, pb_[:, 0:256], act),
                        reads=[pbn_], writes=[dname])

        for (wsb, wname, dst, dname, bcol) in (
                (wa_sb, "wa_sb", za_sb, "za_sb", 0),
                (wb_sb, "wb_sb", zb_sb, "zb_sb", 4)):
            for tch in range(4):
                with p.batch("tensor"):
                    for dc in range(DC):
                        lhs = wsb[:, dc * 4: dc * 4 + 4]
                        rhs = xln[:, dc * TB + tch * 512: dc * TB + tch * 512 + 512]
                        p.emit("tensor", lambda t, lhs=lhs, rhs=rhs, dc=dc: t.matmul(
                            ps[5][0:4, 0:512], lhs, rhs, start=(dc == 0), stop=False),
                            reads=[wname, "xln"], writes=[psn[5]])
                    p.emit("tensor", lambda t, bcol=bcol: t.matmul(
                        ps[5][0:4, 0:512], bab_sb[0:1, bcol:bcol + 4],
                        mk.ones_r[0:1, 0:512], start=False, stop=True),
                        reads=["bab", "ones_r"], writes=[psn[5]])
                d = dst[:, tch * 512: tch * 512 + 512]
                p.emit("vector", lambda v, d=d: v.tensor_copy(d, ps[5][0:4, 0:512]),
                       reads=[psn[5]], writes=[dname])
        p.emit("scalar", lambda s: s.activation(bt_sb[:], zb_sb[:], AF.Sigmoid),
               reads=["zb_sb"], writes=["bt_sb"])
        p.emit("scalar", lambda s: s.activation(lsc[:], za_sb[:], AF.Abs),
               reads=["za_sb"], writes=["lsc"])
        if DBG_DELTA and tb == 0:
            dd1 = p.dram("d_abs", [4, TB], F32, kind="ExternalOutput").ap()
            p.emit("gpsimd", lambda g, dd1=dd1: g.dma_start(out=dd1[:], in_=lsc[:]),
                   reads=["lsc"], writes=["d_abs"], dma=True)
            p.barrier_from("scalar", [("gpsimdD", p.cnt["gpsimdD"])])
        p.emit("scalar", lambda s: s.activation(lsc[:], lsc[:], AF.Exp, scale=-1.0),
               reads=["lsc"], writes=["lsc"])
        if DBG_DELTA and tb == 0:
            dd2 = p.dram("d_exp", [4, TB], F32, kind="ExternalOutput").ap()
            p.emit("gpsimd", lambda g, dd2=dd2: g.dma_start(out=dd2[:], in_=lsc[:]),
                   reads=["lsc"], writes=["d_exp"], dma=True)
            p.barrier_from("scalar", [("gpsimdD", p.cnt["gpsimdD"])])
        one_c = mk.constcol(1.0)
        p.emit("scalar", lambda s, one_c=one_c: s.activation(
            lsc[:], lsc[:], AF.Ln, bias=one_c[0:4, 0:1]),
            reads=["lsc", "cc_1.0"], writes=["lsc"])
        p.emit("vector", lambda v: v.tensor_scalar(lg_sb[:], za_sb[:], 0.0, None,
                                                   op0=ALU.min),
               reads=["za_sb"], writes=["lg_sb"])
        p.emit("vector", lambda v: v.tensor_sub(lg_sb[:], lg_sb[:], lsc[:]),
               reads=["lg_sb", "lsc"], writes=["lg_sb"])
        if DBG_DELTA and tb == 0:
            for nm, til, rd in (("d_za", za_sb[:], "za_sb"), ("d_lsc", lsc[:], "lsc"),
                                ("d_lgpre", lg_sb[:], "lg_sb")):
                dd = p.dram(nm, [4, TB], F32, kind="ExternalOutput").ap()
                p.emit("gpsimd", lambda g, dd=dd, til=til: g.dma_start(out=dd[:], in_=til),
                       reads=[rd], writes=[nm], dma=True)
            p.barrier_from("vector", [("gpsimdD", p.cnt["gpsimdD"])])
        for cc in range(NP):
            seg = lg_sb[:, cc * 128: cc * 128 + 128]
            p.emit("vector", lambda v, seg=seg: v.tensor_tensor_scan(
                seg, mk.ones_f[0:4, 0:128], seg, 0.0, op0=ALU.mult, op1=ALU.add),
                reads=["lg_sb", "ones_f"], writes=["lg_sb"])

        for cc in range(NP):
            tcol = cc * 128
            p.emit("tensor", lambda t, tcol=tcol: t.transpose(
                ps[0][:, 0:4], lg_sb[0:4, tcol:tcol + 128], mk.id_f[0:4, 0:4]),
                reads=["lg_sb", "id_f"], writes=[psn[0]])
            p.emit("vector", lambda v: v.tensor_copy(cb8[:, 0:4], ps[0][0:128, 0:4]),
                   reads=[psn[0]], writes=["cb8"])
            p.emit("tensor", lambda t, tcol=tcol: t.transpose(
                ps[0][:, 0:4], bt_sb[0:4, tcol:tcol + 128], mk.id_f[0:4, 0:4]),
                reads=["bt_sb", "id_f"], writes=[psn[0]])
            p.emit("vector", lambda v: v.tensor_copy(cb8[:, 4:8], ps[0][0:128, 0:4]),
                   reads=[psn[0]], writes=["cb8"])
            p.emit("scalar", lambda s: s.activation(gam4[:], cb8[:, 0:4], AF.Exp),
                   reads=["cb8"], writes=["gam4"])
            HS = list(range(4))

            def hc(h):
                return dict(
                    lcol=cb8[:, h:h + 1], bcol=cb8[:, 4 + h:5 + h],
                    gcol=gam4[:, h:h + 1],
                    kv=bass.AP(k_fm, h * TB + tcol, [[4 * TB, 64], [1, 128]]),
                    qv=bass.AP(q_fm, h * TB + tcol, [[4 * TB, 64], [1, 128]]),
                    vv=bass.AP(v_tm, cc * 256 + h * 64, [[16 * 256, 128], [1, 64]]),
                    gv=bass.AP(g_tm, cc * 256 + h * 64, [[16 * 256, 128], [1, 64]]),
                    zv=zst[0:64, h * 64: h * 64 + 64],
                    zvb=zb16[0:64, h * 64: h * 64 + 64],
                    pb=ps[1 + h], pbn=psn[1 + h],
                    bb=bfb[h % 2][0], bbn=bfb[h % 2][1],
                    cbc=pp4[:, h * 128: h * 128 + 128],
                    lrh=lrt4[:, h * 128: h * 128 + 128],
                    t1a=t14[:, (h % 2) * 256: (h % 2) * 256 + 128],
                    t1b=t14[:, (h % 2) * 256 + 128: (h % 2) * 256 + 256],
                    pph=pp4[:, h * 128: h * 128 + 128],
                    p2h=p24[:, h * 128: h * 128 + 128],
                    ah=a4[:, h * 128: h * 128 + 128],
                    wth=wt4[:, h * 128: h * 128 + 128],
                    xh=x4[:, h * 64: h * 64 + 64],
                    rh=o14[:, (h % 2) * 64: (h % 2) * 64 + 64],
                    kth=ktm4[:, h * 64: h * 64 + 64],
                    kzh=kz4[:, h * 64: h * 64 + 64],
                    o1h=o14[:, h * 64: h * 64 + 64],
                    ogh=og4[:, h * 64: h * 64 + 64],
                    rca=rc4[:, 2 * h: 2 * h + 1], rcb=rc4[:, 2 * h + 1: 2 * h + 2],
                )
            C = [hc(h) for h in HS]
            for h in HS:
                c = C[h]
                p.emit("vector", lambda v, c=c: v.tensor_copy(
                    c["cbc"], c["lcol"].to_broadcast([128, 128])),
                    reads=["cb8"], writes=[f"pp{h}"])
            for h in HS:
                c = C[h]
                p.emit("tensor", lambda t, c=c: t.transpose(
                    c["pb"][:, 0:128], c["cbc"], mk.id_f[:]),
                    reads=[f"pp{h}", "id_f"], writes=[c["pbn"]])
            for h in HS:
                c = C[h]
                p.emit("scalar", lambda s, c=c: s.copy(c["lrh"], c["pb"][:, 0:128]),
                       reads=[c["pbn"]], writes=[f"lrt{h}"])
            for h in HS:
                c = C[h]
                p.emit("vector", lambda v, c=c: v.scalar_tensor_tensor(
                    c["t1a"], c["lrh"], c["lcol"], mstrict[:],
                    op0=ALU.subtract, op1=ALU.subtract),
                    reads=[f"lrt{h}", "cb8", "mstrict"], writes=[f"t1a{h % 2}"])
                p.emit("vector", lambda v, c=c: v.scalar_tensor_tensor(
                    c["t1b"], c["lrh"], c["lcol"], mupper[:],
                    op0=ALU.subtract, op1=ALU.add),
                    reads=[f"lrt{h}", "cb8", "mupper"], writes=[f"t1b{h % 2}"])
                p.emit("scalar", lambda s, c=c: s.activation(
                    c["pph"], c["t1a"], AF.Exp, scale=-1.0),
                    reads=[f"t1a{h % 2}"], writes=[f"pp{h}"])
                p.emit("scalar", lambda s, c=c: s.activation(
                    c["p2h"], c["t1b"], AF.Exp),
                    reads=[f"t1b{h % 2}"], writes=[f"p2{h}"])
            for h in HS:
                c = C[h]
                p.emit("tensor", lambda t, c=c: t.matmul(
                    c["pb"][:, 0:128], c["kv"], c["kv"], start=True, stop=True),
                    reads=["k_fm"], writes=[c["pbn"]])
            for h in HS:
                c = C[h]
                p.emit("vector", lambda v, c=c: v.scalar_tensor_tensor(
                    c["ah"], c["pb"][:, 0:128], c["bcol"], c["pph"],
                    op0=ALU.mult, op1=ALU.mult),
                    reads=[c["pbn"], "cb8", f"pp{h}"], writes=[f"a4_{h}"])
            for h in HS:
                c = C[h]
                p.emit("tensor", lambda t, c=c: t.matmul(
                    c["pb"][:, 0:128], c["kv"], c["qv"], start=True, stop=True),
                    reads=["k_fm", "q_fm"], writes=[c["pbn"]])
            for h in HS:
                c = C[h]
                p.emit("vector", lambda v, c=c: v.scalar_tensor_tensor(
                    c["wth"], c["pb"][:, 0:128], 1.0, c["p2h"],
                    op0=ALU.mult, op1=ALU.mult),
                    reads=[c["pbn"], f"p2{h}"], writes=[f"wt{h}"])
            for h in HS:
                c = C[h]
                p.emit("tensor", lambda t, c=c: t.matmul(
                    c["pb"][:, 0:64], c["kv"], c["zvb"], start=True, stop=True),
                    reads=["k_fm", "zb16"], writes=[c["pbn"]])
            for h in HS:
                c = C[h]
                p.emit("vector", lambda v, c=c: v.tensor_scalar(
                    c["rh"], c["pb"][:, 0:64], c["gcol"], None, op0=ALU.mult),
                    reads=[c["pbn"], "gam4"], writes=[f"o1{h % 2}"])
                p.emit("vector", lambda v, c=c: v.tensor_sub(c["rh"], c["vv"], c["rh"]),
                       reads=["v_tm", f"o1{h % 2}"], writes=[f"o1{h % 2}"])
                p.emit("vector", lambda v, c=c: v.tensor_scalar(
                    c["xh"], c["rh"], c["bcol"], None, op0=ALU.mult),
                    reads=[f"o1{h % 2}", "cb8"], writes=[f"x4_{h}"])
            for h in HS:
                c = C[h]
                p.emit("tensor", lambda t, c=c: t.transpose(
                    c["bb"][:, 0:128], c["ah"], mk.id_b[:]),
                    reads=[f"a4_{h}", "id_b"], writes=[c["bbn"]])
                p.emit("scalar", lambda s, c=c, h=h: s.copy(
                    at4[:, (h * 7) * 128: (h * 7) * 128 + 128], c["bb"][:, 0:128]),
                    reads=[c["bbn"]], writes=[f"at4_{h}"])
            for lvl in range(1, 7):
                for h in HS:
                    c = C[h]
                    atl = at4[:, (h * 7 + lvl - 1) * 128: (h * 7 + lvl - 1) * 128 + 128]
                    cur = c["ah"] if lvl == 1 else a24[:, h * 128: h * 128 + 128]
                    curn = f"a4_{h}" if lvl == 1 else f"a24_{h}"
                    p.emit("tensor", lambda t, c=c, atl=atl, cur=cur: t.matmul(
                        c["pb"][:, 0:128], atl, cur, start=True, stop=True),
                        reads=[f"at4_{h}", curn], writes=[c["pbn"]])
                for h in HS:
                    c = C[h]
                    p.emit("vector", lambda v, c=c, h=h: v.tensor_copy(
                        a24[:, h * 128: h * 128 + 128], c["pb"][:, 0:128]),
                        reads=[c["pbn"]], writes=[f"a24_{h}"])
                for h in HS:
                    c = C[h]
                    p.emit("tensor", lambda t, c=c, h=h: t.transpose(
                        c["bb"][:, 0:128], a24[:, h * 128: h * 128 + 128], mk.id_b[:]),
                        reads=[f"a24_{h}", "id_b"], writes=[c["bbn"]])
                    p.emit("scalar", lambda s, c=c, h=h, lvl=lvl: s.copy(
                        at4[:, (h * 7 + lvl) * 128: (h * 7 + lvl) * 128 + 128],
                        c["bb"][:, 0:128]),
                        reads=[c["bbn"]], writes=[f"at4_{h}"])
            for lvl in (6, 5, 4, 3, 2, 1):
                for h in HS:
                    c = C[h]
                    atl = at4[:, (h * 7 + lvl) * 128: (h * 7 + lvl) * 128 + 128]
                    p.emit("tensor", lambda t, c=c, atl=atl: t.matmul(
                        c["pb"][:, 0:64], atl, c["xh"], start=True, stop=True),
                        reads=[f"at4_{h}", f"x4_{h}"], writes=[c["pbn"]])
                for h in HS:
                    c = C[h]
                    p.emit("vector", lambda v, c=c: v.tensor_add(
                        c["xh"], c["xh"], c["pb"][:, 0:64]),
                        reads=[c["pbn"], f"x4_{h}"], writes=[f"x4_{h}"])
            for h in HS:
                c = C[h]
                atl = at4[:, (h * 7) * 128: (h * 7) * 128 + 128]
                p.emit("tensor", lambda t, c=c, atl=atl: t.matmul(
                    c["pb"][:, 0:64], atl, c["xh"], start=True, stop=True),
                    reads=[f"at4_{h}", f"x4_{h}"], writes=[c["pbn"]])
            for h in HS:
                c = C[h]
                p.emit("vector", lambda v, c=c: v.tensor_sub(
                    c["xh"], c["xh"], c["pb"][:, 0:64]),
                    reads=[c["pbn"], f"x4_{h}"], writes=[f"x4_{h}"])
            for h in HS:
                c = C[h]
                p.emit("tensor", lambda t, c=c: t.matmul(
                    c["pb"][:, 0:64], c["wth"], c["xh"], start=True, stop=True),
                    reads=[f"wt{h}", f"x4_{h}"], writes=[c["pbn"]])
            for h in HS:
                c = C[h]
                p.emit("scalar", lambda s, c=c: s.copy(c["o1h"], c["pb"][:, 0:64]),
                       reads=[c["pbn"]], writes=[f"o1{h}"])
            for h in HS:
                c = C[h]
                p.emit("tensor", lambda t, c=c: t.matmul(
                    c["pb"][:, 0:64], c["qv"], c["zvb"], start=True, stop=True),
                    reads=["q_fm", "zb16"], writes=[c["pbn"]])
            for h in HS:
                c = C[h]
                p.emit("vector", lambda v, c=c: v.scalar_tensor_tensor(
                    c["ogh"], c["pb"][:, 0:64], c["gcol"], c["o1h"],
                    op0=ALU.mult, op1=ALU.add),
                    reads=[c["pbn"], "gam4", f"o1{h}"], writes=[f"og4_{h}"])
            for h in HS:
                c = C[h]
                p.emit("vector", lambda v, c=c: v.tensor_tensor(
                    c["ogh"], c["ogh"], c["gv"], op=ALU.mult),
                    reads=[f"og4_{h}", "g_tm"], writes=[f"og4_{h}"])
            for h in HS:
                c = C[h]
                p.emit("tensor", lambda t, c=c: t.transpose(
                    c["bb"][:, 0:64], c["kv"], mk.id_b[0:64, 0:64]),
                    reads=["k_fm", "id_b"], writes=[c["bbn"]])
                p.emit("scalar", lambda s, c=c: s.copy(c["kth"], c["bb"][:, 0:64]),
                       reads=[c["bbn"]], writes=[f"ktm{h}"])
            for h in HS:
                c = C[h]
                p.emit("vector", lambda v, c=c: v.tensor_scalar(
                    c["rca"], c["lrh"][:, 127:128], c["lcol"], None, op0=ALU.subtract),
                    reads=[f"lrt{h}", "cb8"], writes=[f"rca{h}"])
            for h in HS:
                c = C[h]
                p.emit("scalar", lambda s, c=c: s.activation(c["rca"], c["rca"], AF.Exp),
                       reads=[f"rca{h}"], writes=[f"rca{h}"])
                p.emit("scalar", lambda s, c=c: s.activation(
                    c["rcb"], c["lrh"][:, 127:128], AF.Exp),
                    reads=[f"lrt{h}"], writes=[f"rcb{h}"])
            for h in HS:
                c = C[h]
                p.emit("vector", lambda v, c=c: v.tensor_scalar(
                    c["kzh"], c["kth"], c["rca"], None, op0=ALU.mult),
                    reads=[f"ktm{h}", f"rca{h}"], writes=[f"kz{h}"])
            for h in HS:
                c = C[h]
                p.emit("tensor", lambda t, c=c: t.matmul(
                    c["pb"][0:64, 0:64], c["kzh"], c["xh"], start=True, stop=True),
                    reads=[f"kz{h}", f"x4_{h}"], writes=[c["pbn"]])
            for h in HS:
                c = C[h]
                p.emit("vector", lambda v, c=c: v.scalar_tensor_tensor(
                    c["zv"], c["zv"], c["rcb"][0:64, 0:1], c["pb"][0:64, 0:64],
                    op0=ALU.mult, op1=ALU.add),
                    reads=[f"rcb{h}", "zst", c["pbn"]], writes=["zst"])
            for h in HS:
                c = C[h]
                p.emit("vector", lambda v, c=c: v.tensor_copy(c["zvb"], c["zv"]),
                       reads=["zst"], writes=["zb16"])
            for h in HS:
                c = C[h]
                p.emit("tensor", lambda t, c=c: t.transpose(
                    c["pb"][0:64, 0:128], c["ogh"], mk.id_f[:]),
                    reads=[f"og4_{h}", "id_f"], writes=[c["pbn"]])
            for h in HS:
                c = C[h]
                p.emit("vector", lambda v, c=c, h=h: v.tensor_copy(
                    ogst[0:64, h * 128: h * 128 + 128], c["pb"][0:64, 0:128]),
                    reads=[c["pbn"]], writes=["ogst"])
            ch = tb * NP + cc
            ocol = ((ch % 4) * TPC + (ch // 4) * 128) if og_perm else (tb * TB + cc * 128)
            p.emit("sync", lambda s, ocol=ocol: s.dma_start(
                out=bass.AP(og_d.tensor, ocol,
                            [[SEQ, 64], [64 * SEQ, 4], [1, 128]]),
                in_=bass.AP(ogst, 0, [[4 * 128, 64], [128, 4], [1, 128]])),
                reads=["ogst"], writes=[f"ogout{tb}_{cc}"], dma=True)
    p.barrier_from("gpsimd", [("syncD", p.cnt["syncD"])])
    p.barrier_from("sync", [("syncD", p.cnt["syncD"])])


def run_delta_scan(xbs, Wq, Wk, Wv, Wa, ba, Wb, bb, Wg):
    nc = get_program("delta_scan", build_delta_scan)
    in_maps = []
    for c in range(NCORE):
        b = c // 4
        h0 = 4 * (c % 4)
        rs = slice(64 * h0, 64 * h0 + 256)

        def padqk(W):
            Wt = W[rs].T
            out = np.zeros((D, 512), np.float32)
            for hh in range(4):
                out[:, 128 * hh:128 * hh + 64] = Wt[:, 64 * hh:64 * hh + 64]
            return out
        in_maps.append({
            "x_in": xbs[b],
            "wq": padqk(Wq), "wk": padqk(Wk),
            "wv": np.ascontiguousarray(Wv[rs].T), "wg": np.ascontiguousarray(Wg[rs].T),
            "wa": np.ascontiguousarray(Wa[h0:h0 + 4].T),
            "wb": np.ascontiguousarray(Wb[h0:h0 + 4].T),
            "ba": ba[h0:h0 + 4].reshape(1, 4), "bb": bb[h0:h0 + 4].reshape(1, 4),
        })
    res = run_bass_kernel_spmd(nc, in_maps, core_ids=list(range(NCORE)))
    return [res.results[c]["og"] for c in range(NCORE)]


def build_delta_proj(mk, xin=None, xout=None, og_ag=None, wsh=None, wrec=None):
    nc, p = mk.nc, mk.p
    def win(name, shape, dtype=F32):
        if wsh is not None and name in wsh:
            return wsh[name]
        apx = p.dram(name, shape, dtype, kind="ExternalInput").ap()
        if wrec is not None:
            wrec[name] = apx
        return apx

    T = TPC
    x_in = xin if xin is not None else p.dram("x_in", [D, T], F32, kind="ExternalInput").ap()
    og_in = None if og_ag is not None else p.dram("og_in", [D, T], F32,
                                                 kind="ExternalInput").ap()
    mofs_d = win("mofs_in", [1, 1], I32) if og_ag is not None else None
    wo_d = win("wo", [D, D], F32)
    x_out = xout if xout is not None else p.dram("x_out", [D, T], F32, kind="ExternalOutput").ap()

    o_fm = p.sbuf("o_fm", [128, DC * T], BF16)
    w_sb = p.sbuf("w_sb", [128, DC * D], BF16)
    stf = p.sbuf("stf", [128, 2 * 512], F32)

    ps = [p.psum(f"ps{i}", [128, 512]) for i in range(8)]
    psn = [f"ps{i}" for i in range(8)]

    mk.setup_consts()
    if not hasattr(mk, "xt"):
        mk.xt = p.sbuf("xt", [128, 2048], F32, persistent=True)
    xt = mk.xt
    if og_ag is None:
        p.emit("gpsimd", lambda g: g.dma_start(out=o_fm[:], in_=_rearr_fm(og_in)),
               writes=["o_fm"], dma=True)
    else:
        mofs_sb = p.sbuf("mofs_sb", [1, 1], I32)
        p.emit("gpsimd", lambda g: g.dma_start(out=mofs_sb[:], in_=mofs_d[:]),
               writes=["mofs_sb"], dma=True)

        def oload(g):
            mo = mk.nc.values_load(mofs_sb[0:1, 0:1], engines=[ET.Pool],
                                   min_val=0, max_val=3 * TPC,
                                   skip_runtime_bounds_check=True)
            return g.dma_start(
                out=o_fm[:],
                in_=bass.AP(og_ag.tensor, mo, [[S, 128], [128 * S, DC], [1, TPC]]))
        p.emit("gpsimd", oload, reads=["mofs_sb"], writes=["o_fm"], dma=True)
    p.emit("gpsimd", lambda g: g.dma_start(
        out=w_sb[:], in_=bass.AP(wo_d.tensor, 0, [[D, 128], [128 * D, DC], [1, D]])),
        writes=["w_sb"], dma=True)

    def evac_o(oc, tch, psb, psname):
        sl = oc % 2
        xs = xt[:, sl * 512: sl * 512 + 512]
        p.emit("sync", lambda s, oc=oc, tch=tch, xs=xs: s.dma_start(
            out=xs, in_=bass.AP(x_in.tensor, (oc * 128) * T + tch * 512,
                                [[T, 128], [1, 512]])),
            writes=[f"xt{sl}"], dma=True)
        osl = stf[:, sl * 512: sl * 512 + 512]
        osn = f"stf{sl}"
        p.emit("vector", lambda v, osl=osl, psb=psb, xs=xs: v.tensor_add(
            osl, psb[:, 0:512], xs),
            reads=[psname, f"xt{sl}"], writes=[osn])
        p.emit("sync", lambda s, osl=osl, oc=oc, tch=tch: s.dma_start(
            out=bass.AP(x_out.tensor, (oc * 128) * T + tch * 512, [[T, 128], [1, 512]]),
            in_=osl),
            reads=[osn], writes=[f"xo{oc}_{tch}"], dma=True)
    mk.proj_fm(o_fm, "o_fm", w_sb, "w_sb", T, D, ps[0:4], psn[0:4], evac_o)
    p.barrier_from("gpsimd", [("syncD", p.cnt["syncD"])])


def run_delta_proj(xs, ogs, Wo):
    nc = get_program("delta_proj", build_delta_proj)
    og_full = [np.concatenate([ogs[4 * b + i] for i in range(4)], axis=0)
               for b in range(2)]
    woT = np.ascontiguousarray(Wo.T)
    in_maps = []
    for c in range(NCORE):
        b = c // 4
        rows = core_rows(c)
        in_maps.append({
            "x_in": xs[c],
            "og_in": np.ascontiguousarray(og_full[b][:, rows]),
            "wo": woT,
        })
    res = run_bass_kernel_spmd(nc, in_maps, core_ids=list(range(NCORE)))
    return [res.results[c]["x_out"] for c in range(NCORE)]



def kernel(
    L0_attn_a_embeddings, L0_attn_a_W_Q, L0_attn_a_W_K, L0_attn_a_W_V,
    L0_attn_b_WQ, L0_attn_b_bQ, L0_attn_b_WK, L0_attn_b_bK, L0_attn_b_WV,
    L0_attn_b_bV, L0_attn_b_WO, L0_attn_b_bO, L0_attn_b_rel_pos_table,
    L0_ffn_gate_w, L0_ffn_expert_bias, L0_ffn_w_gate, L0_ffn_w_up, L0_ffn_w_down,
    L0_ffn_shared_gate_w, L0_ffn_shared_up_w, L0_ffn_shared_down_w,
    L1_attn_a_Wq, L1_attn_a_Wk, L1_attn_a_Wv, L1_attn_a_Wa, L1_attn_a_ba,
    L1_attn_a_Wb, L1_attn_a_bb, L1_attn_a_Wg, L1_attn_a_Wo,
    L1_attn_b_W_qkv, L1_attn_b_b_qkv, L1_attn_b_W_out, L1_attn_b_b_out,
    L1_ffn_Wr, L1_ffn_br, L1_ffn_W_up, L1_ffn_b_up, L1_ffn_W_down, L1_ffn_b_down,
):
    x = np.asarray(L0_attn_a_embeddings, dtype=np.float32)
    xs = split_x(x)

    def ffn0(xs):
        return run_ffn0(xs, np.asarray(L0_ffn_gate_w), np.asarray(L0_ffn_expert_bias),
                        np.asarray(L0_ffn_w_gate), np.asarray(L0_ffn_w_up),
                        np.asarray(L0_ffn_w_down), np.asarray(L0_ffn_shared_gate_w),
                        np.asarray(L0_ffn_shared_up_w), np.asarray(L0_ffn_shared_down_w))

    def ffn1(xs):
        return run_ffn1(xs, np.asarray(L1_ffn_Wr), np.asarray(L1_ffn_br),
                        np.asarray(L1_ffn_W_up), np.asarray(L1_ffn_b_up),
                        np.asarray(L1_ffn_W_down), np.asarray(L1_ffn_b_down))

    xs = run_attn_a(xs, np.asarray(L0_attn_a_W_Q), np.asarray(L0_attn_a_W_K),
                    np.asarray(L0_attn_a_W_V))
    xs = ffn0(xs)
    xs = run_attn_b(xs, np.asarray(L0_attn_b_WQ), np.asarray(L0_attn_b_bQ),
                    np.asarray(L0_attn_b_WK), np.asarray(L0_attn_b_bK),
                    np.asarray(L0_attn_b_WV), np.asarray(L0_attn_b_bV),
                    np.asarray(L0_attn_b_WO), np.asarray(L0_attn_b_bO),
                    np.asarray(L0_attn_b_rel_pos_table))
    xs = ffn0(xs)
    xfull = merge_x(xs)
    xb_fm = [np.ascontiguousarray(xfull[b].T) for b in range(B)]
    ogs = run_delta_scan(xb_fm, np.asarray(L1_attn_a_Wq), np.asarray(L1_attn_a_Wk),
                         np.asarray(L1_attn_a_Wv), np.asarray(L1_attn_a_Wa),
                         np.asarray(L1_attn_a_ba), np.asarray(L1_attn_a_Wb),
                         np.asarray(L1_attn_a_bb), np.asarray(L1_attn_a_Wg))
    xs = run_delta_proj(xs, ogs, np.asarray(L1_attn_a_Wo))
    xs = ffn1(xs)
    xs = run_attn_b1(xs, np.asarray(L1_attn_b_W_qkv), np.asarray(L1_attn_b_b_qkv),
                     np.asarray(L1_attn_b_W_out), np.asarray(L1_attn_b_b_out))
    xs = ffn1(xs)
    return merge_x(xs)



FUSED_DBG = False


def build_fused(mk):
    nc, p = mk.nc, mk.p
    p.prefix = "g_"
    mk.setup_consts()
    LNStream(mk, TPC)
    mk.constcol(1e-5)
    mk.constcol(1.0)
    p.stage_end()
    p.prefix = ""
    xin0 = p.dram("xin0", [D, TPC], F32, kind="ExternalInput").ap()
    xfin = p.dram("xfin", [D, TPC], F32, kind="ExternalOutput").ap()
    xd = [p.dram(f"xd{i}", [D, TPC], F32).ap() for i in range(8)]
    xag = p.dram("xag", [4 * D, TPC], F32).ap()
    og_d = p.dram("ogd", [256, S], F32).ap()
    ogag = p.dram("ogag", [4 * 256, S], F32).ap()

    w0 = {}
    w1 = {}

    p.prefix = "s0_"
    build_attn_a(mk, xin=xin0, xout=xd[0])
    p.stage_end()

    p.prefix = "s1_"
    build_ffn0(mk, xin=xd[0], xout=xd[1], wrec=w0)
    p.stage_end()

    p.prefix = "s2_"
    build_attn_b(mk, xin=xd[1], xout=xd[2])
    p.stage_end()

    p.prefix = "s3_"
    build_ffn0(mk, xin=xd[2], xout=xd[3], wsh=w0)
    p.stage_end()

    p.prefix = "s4_"
    tok = p.emit("gpsimd", lambda g: g.collective_compute(
        "AllGather", ALU.bypass, replica_groups=GROUPS, ins=[xd[3][:]], outs=[xag[:]]),
        writes=["xagw"], coll=True)
    p.barrier_from("sync", [tok])
    build_delta_scan(mk, xin_ag=xag, og_out=og_d, og_perm=True)
    p.emit("gpsimd", lambda g: g.collective_compute(
        "AllGather", ALU.bypass, replica_groups=GROUPS, ins=[og_d[:]], outs=[ogag[:]]),
        writes=["ogagw"], coll=True)
    p.stage_end()

    p.prefix = "s5_"
    build_delta_proj(mk, xin=xd[3], xout=xd[4], og_ag=ogag)
    p.stage_end()

    p.prefix = "s6_"
    build_ffn1(mk, xin=xd[4], xout=xd[5], wrec=w1)
    p.stage_end()

    p.prefix = "s7_"
    build_attn_b1(mk, xin=xd[5], xout=xd[6])
    p.stage_end()

    p.prefix = "s8_"
    build_ffn1(mk, xin=xd[6], xout=xfin, wsh=w1)
    p.stage_end()
    if FUSED_DBG:
        p.prefix = "dbg_"
        for i in range(8):
            dmp = p.dram(f"xdump{i}", [D, TPC], F32, kind="ExternalOutput").ap()
            p.emit("gpsimd", lambda g, i=i, dmp=dmp: g.dma_start(out=dmp[:], in_=xd[i][:]),
                   writes=[f"xdump{i}"], dma=True)
        dmp = p.dram("ogdump", [4 * 256, S], F32, kind="ExternalOutput").ap()
        p.emit("gpsimd", lambda g, dmp=dmp: g.dma_start(out=dmp[:], in_=ogag[:]),
               writes=["ogdump"], dma=True)
        p.stage_end()
    p.prefix = ""


def kernel(
    L0_attn_a_embeddings, L0_attn_a_W_Q, L0_attn_a_W_K, L0_attn_a_W_V,
    L0_attn_b_WQ, L0_attn_b_bQ, L0_attn_b_WK, L0_attn_b_bK, L0_attn_b_WV,
    L0_attn_b_bV, L0_attn_b_WO, L0_attn_b_bO, L0_attn_b_rel_pos_table,
    L0_ffn_gate_w, L0_ffn_expert_bias, L0_ffn_w_gate, L0_ffn_w_up, L0_ffn_w_down,
    L0_ffn_shared_gate_w, L0_ffn_shared_up_w, L0_ffn_shared_down_w,
    L1_attn_a_Wq, L1_attn_a_Wk, L1_attn_a_Wv, L1_attn_a_Wa, L1_attn_a_ba,
    L1_attn_a_Wb, L1_attn_a_bb, L1_attn_a_Wg, L1_attn_a_Wo,
    L1_attn_b_W_qkv, L1_attn_b_b_qkv, L1_attn_b_W_out, L1_attn_b_b_out,
    L1_ffn_Wr, L1_ffn_br, L1_ffn_W_up, L1_ffn_b_up, L1_ffn_W_down, L1_ffn_b_down,
):
    A = np.asarray
    x = A(L0_attn_a_embeddings, dtype=np.float32)
    xs = split_x(x)
    nc = get_program("fused", build_fused)

    NFC, E1 = 22, 8
    bu_fm = np.zeros((128, 4 * NFC), np.float32)
    bup = A(L1_ffn_b_up)
    for gi in range(4):
        for fc in range(NFC):
            w = 42 if fc == NFC - 1 else 128
            bu_fm[0:w, gi * NFC + fc] = bup[gi, fc * 128: fc * 128 + w]
    bq_fm2 = np.ascontiguousarray(A(L0_attn_b_bQ).reshape(DC, 128).T)
    bk_fm2 = np.ascontiguousarray(A(L0_attn_b_bK).reshape(DC, 128).T)
    bo_fm2 = np.ascontiguousarray(A(L0_attn_b_bO).reshape(DC, 128).T)
    bvr2 = A(L0_attn_b_bV).reshape(1, D)
    bqkv = A(L1_attn_b_b_qkv)
    bq_fm7 = np.ascontiguousarray(bqkv[0:D].reshape(DC, 128).T)
    bk_fm7 = np.ascontiguousarray(bqkv[D:2 * D].reshape(DC, 128).T)
    bvr7 = bqkv[2 * D:3 * D].reshape(1, D)
    bo_fm7 = np.ascontiguousarray(A(L1_attn_b_b_out).reshape(DC, 128).T)
    Wqd, Wkd = A(L1_attn_a_Wq), A(L1_attn_a_Wk)
    in_maps = []
    for c in range(NCORE):
        m = c % 4
        h0 = 4 * m
        rs = slice(64 * h0, 64 * h0 + 256)

        def padqk(W):
            Wt = W[rs].T
            out = np.zeros((D, 512), np.float32)
            for hh in range(4):
                out[:, 128 * hh:128 * hh + 64] = Wt[:, 64 * hh:64 * hh + 64]
            return np.ascontiguousarray(out)
        tabs2, ctab2 = attn_b_tables(c, A(L0_attn_b_rel_pos_table))
        wt7, wt07 = attn_b1_tables(c)
        im = {
            "xin0": xs[c],
            "s0_wq": A(L0_attn_a_W_Q), "s0_wk": A(L0_attn_a_W_K),
            "s0_wv": A(L0_attn_a_W_V), "s0_mtab": attn_a_masktab(c),
            "s1_gwT": np.ascontiguousarray(A(L0_ffn_gate_w).T),
            "s1_ebias": A(L0_ffn_expert_bias).reshape(1, 16),
            "s1_wgT": np.ascontiguousarray(np.transpose(A(L0_ffn_w_gate), (0, 2, 1))),
            "s1_wuT": np.ascontiguousarray(np.transpose(A(L0_ffn_w_up), (0, 2, 1))),
            "s1_wdT": np.ascontiguousarray(np.transpose(A(L0_ffn_w_down), (0, 2, 1))),
            "s1_sgT": np.ascontiguousarray(A(L0_ffn_shared_gate_w).T),
            "s1_suT": np.ascontiguousarray(A(L0_ffn_shared_up_w).T),
            "s1_sdT": np.ascontiguousarray(A(L0_ffn_shared_down_w).T),
            "s2_wq": np.ascontiguousarray(A(L0_attn_b_WQ).T),
            "s2_wk": np.ascontiguousarray(A(L0_attn_b_WK).T),
            "s2_wv": np.ascontiguousarray(A(L0_attn_b_WV).T),
            "s2_wo": np.ascontiguousarray(A(L0_attn_b_WO).T),
            "s2_bq": bq_fm2, "s2_bk": bk_fm2, "s2_bvr": bvr2, "s2_bo": bo_fm2,
            "s2_rtab": tabs2, "s2_ctab": ctab2,
            "s4_wq": padqk(Wqd), "s4_wk": padqk(Wkd),
            "s4_wv": np.ascontiguousarray(A(L1_attn_a_Wv)[rs].T),
            "s4_wg": np.ascontiguousarray(A(L1_attn_a_Wg)[rs].T),
            "s4_wa": np.ascontiguousarray(A(L1_attn_a_Wa)[h0:h0 + 4].T),
            "s4_wb": np.ascontiguousarray(A(L1_attn_a_Wb)[h0:h0 + 4].T),
            "s4_ba": A(L1_attn_a_ba)[h0:h0 + 4].reshape(1, 4),
            "s4_bb": A(L1_attn_a_bb)[h0:h0 + 4].reshape(1, 4),
            "s5_wo": np.ascontiguousarray(A(L1_attn_a_Wo).T),
            "s5_mofs_in": np.array([[m * TPC]], np.int32),
            "s6_wr": A(L1_ffn_Wr), "s6_br": A(L1_ffn_br).reshape(1, E1),
            "s6_wu": A(L1_ffn_W_up), "s6_bu": bu_fm,
            "s6_wd": A(L1_ffn_W_down), "s6_bd": A(L1_ffn_b_down),
            "s7_wqkv": A(L1_attn_b_W_qkv),
            "s7_bq": bq_fm7, "s7_bk": bk_fm7, "s7_bvr": bvr7, "s7_bo": bo_fm7,
            "s7_wo": A(L1_attn_b_W_out),
            "s7_wtab": wt7, "s7_wtab0": wt07,
            "s7_kofs": np.array([[((m - j) % 4) * 2097152 for j in range(3)]], np.int32),
            "s7_uofs": np.array([[128 if m >= j else 256 for j in range(3)]], np.int32),
        }
        in_maps.append(im)
    res = run_bass_kernel_spmd(nc, in_maps, core_ids=list(range(NCORE)))
    return merge_x([res.results[c]["xfin"] for c in range(NCORE)])
